# version 51
# baseline (speedup 1.0000x reference)
import numpy as np
import concourse.bass as bass
import concourse.mybir as mybir
from concourse.bass_utils import run_bass_kernel_spmd
from contextlib import ExitStack

F32 = mybir.dt.float32
F16 = mybir.dt.float16
AF = mybir.ActivationFunctionType
ALU = mybir.AluOpType

T, E, H, D = 2048, 1024, 16, 64
NP = 128
NTT = T // NP
ECH = E // NP
NTC = 4
NPAIR = H // 2


class Streams:
    ENGS = ("sync", "tensor", "vector", "scalar", "gpsimd")

    def __init__(self):
        self.ops = {k: [] for k in self.ENGS}
        self.cnt = {}

    def emit(self, q, fn, waits=(), inc=None, amt=1):
        ws = tuple((s, v) for (s, v) in waits if v is not None and v > 0)
        if q == "vector" and inc is None:
            inc = "sDVE"
        if inc is None:
            self.ops[q].append((fn, ws, None, 0, None))
            return None
        self.cnt[inc] = self.cnt.get(inc, 0) + amt
        val = self.cnt[inc]
        if q == "vector":
            prev = getattr(self, "_vprev", None)
            if prev is not None:
                ws = ws + ((prev[0], prev[1]),)
            self._vprev = (inc, val) if inc == "sDVE" else prev
        self.ops[q].append((fn, ws, inc, amt, val))
        return val


def build_program(debug=False):
    nc = bass.Bass("TRN2", target_bir_lowering=False)

    x_d = nc.dram_tensor("x", [T, E], F16, kind="ExternalInput").ap()
    wqT_d = nc.dram_tensor("wqT", [E, E], F16, kind="ExternalInput").ap()
    wkT_d = nc.dram_tensor("wkT", [E, E], F16, kind="ExternalInput").ap()
    wvT_d = nc.dram_tensor("wvT", [E, E], F16, kind="ExternalInput").ap()
    woT_d = nc.dram_tensor("woT", [E, E], F16, kind="ExternalInput").ap()
    wT_d = nc.dram_tensor("wT", [E, E], F32, kind="ExternalInput").ap()
    disha_d = nc.dram_tensor("disha", [64, E], F32, kind="ExternalInput").ap()
    bq_d = nc.dram_tensor("bq2", [16, NP], F16, kind="ExternalInput").ap()
    bk_d = nc.dram_tensor("bk2", [16, NP], F16, kind="ExternalInput").ap()
    bv_d = nc.dram_tensor("bv", [1, E], F16, kind="ExternalInput").ap()
    bo_d = nc.dram_tensor("bo", [1, E], F16, kind="ExternalInput").ap()
    tri_d = nc.dram_tensor("tri", [NP, NP], F16, kind="ExternalInput").ap()
    eye_d = nc.dram_tensor("eye", [NP, NP], F32, kind="ExternalInput").ap()
    hsel_d = nc.dram_tensor("hsel", [16, E], F16, kind="ExternalInput").ap()
    out_d = nc.dram_tensor("out", [T, E], F32, kind="ExternalOutput").ap()
    dbg = {}
    if debug:
        for nm, shp, dt in [("dbg_xnT", [NP, ECH, T], F16), ("dbg_QT", [NP, ECH, T], F16),
                            ("dbg_KT", [NP, ECH, T], F16), ("dbg_V", [NP, NTT, 1040], F16),
                            ("dbg_g", [NP, NTT, E], F16), ("dbg_AOT", [NP, ECH, T], F16)]:
            dbg[nm] = nc.dram_tensor(nm, shp, dt, kind="ExternalOutput").ap()

    base = (nc.sbuf_base + 31) // 32 * 32
    top = nc.sbuf_top

    def at(name, shape, dtype, off):
        per = int(np.prod(shape[1:])) * mybir.dt.size(dtype)
        assert base + off + per <= top, (name, off, per, top - base)
        return nc.alloc_sbuf_tensor_at(name, shape, dtype, offset=base + off)

    o_xnT = 0
    o_QT = 32768
    o_KT = 65536
    o_V = 98304
    o_g = 131584
    o_Wf = 164352
    o_ws = 180736
    o_xn = 188928
    o_xin = 197120
    o_misc = 201216

    xnT = at("xnT", [NP, ECH, T], F16, o_xnT)
    AttnOT = at("AttnOT", [NP, ECH, T], F16, o_xnT)
    QT = at("QT", [NP, ECH, T], F16, o_QT)
    KT = at("KT", [NP, ECH, T], F16, o_KT)
    esb = [at(f"esb{i}", [NP, 512], F32, o_ws + 4096 + 2048 * i) for i in range(2)]
    Vp = at("Vp", [NP, NTT, 1040], F16, o_V)
    g_sb = at("g", [NP, NTT, E], F16, o_g)
    Weff = at("Weff", [NP, ECH, E], F16, o_Wf)
    woT = at("woT", [NP, ECH, E], F16, o_Wf)
    wsb = [at("wsb0", [NP, ECH, 512], F16, o_ws), at("wsb1", [NP, ECH, 512], F16, o_xn)]
    wbuf = [at(f"wbuf{i}", [NP, E], F32, o_ws + 4096 * i) for i in range(2)]
    L16 = at("L16", [NP, 4, 512], F16, o_ws)
    xn_sb = [at(f"xn{i}", [NP, E], F32, o_xn + 4096 * i) for i in range(2)]
    PT = [at(f"PT{i}", [NP, 2, 512], F16, o_xn + 2048 * i) for i in range(4)]
    xin = ([at(f"xin{i}", [NP, E], F16, o_xin + 2048 * i) for i in range(2)]
           + [at(f"xin{i+2}", [NP, E], F16, o_ws + 2048 * i) for i in range(2)]
           + [at(f"xin{i+4}", [NP, E], F16, o_QT + 16384 + 2048 * i) for i in range(4)])
    D2 = at("D2", [NP, E], F32, o_xin)
    wbig = at("wbig", [NP, 4, E], F32, o_KT + 16384)

    mo = o_misc

    def misc(name, shape, dtype):
        nonlocal mo
        h = at(name, shape, dtype, mo)
        per = int(np.prod(shape[1:])) * mybir.dt.size(dtype)
        mo += (per + 31) // 32 * 32
        return h

    tri_sb = misc("tri", [NP, NP], F16)
    eye_sb = misc("eye", [NP, NP], F32)
    ones128 = misc("ones128", [1, NP], F16)
    neg2 = misc("neg2", [NP, 1], F32)
    s125 = misc("s125", [NP, 1], F32)
    bqc = misc("bqc", [NP, 16], F16)
    bkc = misc("bkc", [NP, 16], F16)
    bqcs = misc("bqcs", [NP, ECH], F32)
    bkcs = misc("bkcs", [NP, ECH], F32)
    bv_off = mo
    bv_sb = misc("bv_sb", [1, E], F16)
    bo_sb = misc("bo_sb", [1, E], F16)
    OddB = [at("OddB0", [NP, 512], F16, bv_off)]
    neg1 = misc("neg1", [NP, 1], F32)
    epsc = misc("epsc", [NP, 1], F32)
    st6 = [misc(f"st6_{i}", [NP, 12], F32) for i in range(4)]
    mv = [misc(f"mv_{i}", [NP, 2], F32) for i in range(4)]
    stdt = [misc(f"std_{i}", [NP, 1], F32) for i in range(4)]
    rstd = [misc(f"rstd_{i}", [NP, 1], F32) for i in range(4)]
    nb = [misc(f"nb_{i}", [NP, 1], F32) for i in range(4)]
    hsel = misc("hsel", [16, E], F16)
    OddA = misc("OddA", [NP, 512], F16)
    PT.append(at("PT4", [NP, 2, 512], F16, o_xin))
    PT.append(at("PT5", [NP, 2, 512], F16, o_xin + 2048))
    PT.append(at("PT6", [NP, 2, 512], F16, o_ws + 2048))
    assert base + mo <= top, (mo, top - base)

    S = Streams()
    SEMS = ("sX0", "sX1", "sX2", "sX3", "sW0", "sW1", "sWO", "sCst", "sD2",
            "sSH0", "sSH1", "sO0", "sO1", "sO2", "sO3", "sB0", "sB1", "sB2", "sB3", "sLA",
            "sLB", "sDBG", "sPE", "sACT", "sDVE", "sGP")

    with ExitStack() as ctx:
        e = ctx.enter_context
        PS6 = e(nc.psum_tensor("PS6", [NP, 6, 512], F32))
        Q2 = e(nc.psum_tensor("Q2", [NP, 2, 512], F32))
        for h in (PS6, Q2):
            assert nc.lookup_mloc(h).addr % 2048 == 0, nc.lookup_mloc(h).addr
        psC = [PS6[:, i, :] for i in range(4)]
        psE = psC
        sems = {n: e(nc.semaphore(n)) for n in SEMS}
        block = e(nc.Block())

        def dma(q, out_ap, in_ap, waits=(), transpose=False, sem="sIN"):
            if transpose:
                return S.emit(q, lambda g, o=out_ap, i=in_ap: g.dma_start(out=o, in_=i, transpose=True),
                              waits, sem, 16)
            return S.emit(q, lambda g, o=out_ap, i=in_ap: g.dma_start(out=o, in_=i), waits, sem, 16)

        dma("sync", eye_sb[:], eye_d, sem="sCst")
        dma("sync", tri_sb[:], tri_d, sem="sCst")
        dma("sync", bqc[:], bq_d, transpose=True, sem="sCst")
        dma("sync", bkc[:], bk_d, transpose=True, sem="sCst")
        dma("sync", bv_sb[:], bv_d, sem="sCst")
        c_cst = dma("sync", bo_sb[:], bo_d, sem="sCst")

        S.emit("vector", lambda g: g.memset(ones128[:], 1.0))
        S.emit("vector", lambda g: g.memset(neg2[:], -2.0))
        S.emit("vector", lambda g: g.memset(neg1[:], -1.0))
        S.emit("vector", lambda g: g.memset(epsc[:], 1e-5))
        S.emit("vector", lambda g: g.memset(s125[:], 0.125))
        vones = bass.AP(Vp.ap().tensor, Vp.ap().offset + 64,
                        [list(Vp.ap().ap[0]), [1040, NTT], [65, H], [1, 1]])
        v_vones = S.emit("vector", lambda g: g.memset(vones, 1.0), inc="sDVE")
        v_bqcs = S.emit("vector", lambda g: g.tensor_scalar_mul(bqcs[:], bqc[:, 0:ECH], 0.125),
                        waits=[("sCst", c_cst)], inc="sDVE")
        v_bkcs = S.emit("vector", lambda g: g.tensor_copy(bkcs[:], bkc[:, 0:ECH]),
                        waits=[("sCst", c_cst)], inc="sDVE")

        a_apply = [None] * NTT
        v_xnT = [None] * NTT
        v_stats = [None] * NTT
        c_x = [None] * NTT
        p_tp = [None] * NTT
        a_sqrt = [None] * NTT
        v_nb = [None] * NTT

        a_vr = [None] * NTT
        a_v1 = [None] * NTT

        def a_load(tt):
            par = tt % 8
            c_x[tt] = dma("sync", xin[par][:], x_d[tt * NP:(tt + 1) * NP, :],
                          waits=[("sDVE", a_apply[tt - 8] if tt >= 8 else None),
                                 ("sDVE", a_v1[tt - 4] if tt >= 4 else None)],
                          sem=f"sX{par % 4}")

        def a_stats(tt):
            par = tt % 4
            xp = tt % 8
            S.emit("vector", lambda g, p=par, x=xp: g.bn_stats(st6[p][:, 0:6], xin[x][:, 0:512]),
                   waits=[(f"sX{xp % 4}", c_x[tt])])
            v1 = S.emit("vector", lambda g, p=par, x=xp: g.bn_stats(st6[p][:, 6:12], xin[x][:, 512:1024]),
                        inc="sDVE")
            a_v1[tt] = v1
            v_stats[tt] = S.emit("vector", lambda g, p=par: g.bn_aggr(mv[p][:], st6[p][:]),
                                 waits=[("sDVE", v1),
                                        ("sACT", a_sqrt[tt - 4] if tt >= 4 else None)],
                                 inc="sDVE")

        def a_sqrt_f(tt):
            par = tt % 4
            S.emit("vector", lambda g, p=par: g.tensor_scalar_add(
                stdt[p][:], mv[p][:, 1:2], epsc[:, 0:1]),
                waits=[("sACT", a_sqrt[tt - 4] if tt >= 4 else None)])
            vrc = S.emit("vector", lambda g, p=par: g.reciprocal(stdt[p][:], stdt[p][:]),
                         inc="sDVE")
            a_sqrt[tt] = S.emit("scalar", lambda g, p=par: g.activation(
                rstd[p][:], stdt[p][:], AF.Sqrt),
                waits=[("sDVE", vrc),
                       ("sDVE", a_apply[tt - 4] if tt >= 4 else None)], inc="sACT")

        def a_apply_f(tt):
            par = tt % 2
            a_apply[tt] = S.emit("vector", lambda g, p=par, p4=tt % 4, x=tt % 8:
                                 g.tensor_scalar(xn_sb[p][:], xin[x][:],
                                                 mv[p4][:, 0:1], rstd[p4][:, 0:1],
                                                 ALU.subtract, ALU.mult),
                waits=[("sACT", a_sqrt[tt]),
                       ("sPE", p_tp[tt - 2] if tt >= 2 else None)], inc="sDVE")

        def a_transp(tt):
            for j in range(ECH):
                p_tp[tt] = S.emit("tensor", lambda g, jj=j, p=tt % 2: g.transpose(
                    PS6[:, 4 + jj // 4, (jj % 4) * NP:(jj % 4 + 1) * NP],
                    xn_sb[p][:, jj * NP:(jj + 1) * NP], eye_sb[:]),
                    waits=([("sDVE", a_apply[tt]), ("sCst", c_cst),
                            ("sACT", v_xnT[tt - 1] if tt >= 1 else None)] if j == 0 else []),
                    inc=("sPE" if j == ECH - 1 else None))

        def a_copy(tt):
            src = PS6.ap().rearrange("p a (b c) -> p (a b) c", c=NP)[:, 16:24, :]
            v_xnT[tt] = S.emit("scalar", lambda g, s=src, t=tt: g.copy(
                xnT[:, :, t * NP:(t + 1) * NP], s), waits=[("sPE", p_tp[tt])], inc="sACT")

        v_weff = [None] * ECH
        c_wT = [None] * ECH
        c_d2b = [None]
        wsrc = [wqT_d, wqT_d, wkT_d, wkT_d, wvT_d, wvT_d]
        c_w = [None] * 6
        p_wdone = [None] * 6
        evac_psC = [None, None, None, None]
        gi = 0

        def mk_wT(ec):
            c_wT[ec] = S.emit("sync", lambda g, e=ec: g.dma_start(
                out=wbig[:, e % 4, :], in_=wT_d[e * NP:(e + 1) * NP, :]),
                waits=[("sDVE", v_weff[ec - 4] if ec >= 4 else None)],
                inc=f"sB{ec % 4}", amt=16)

        def mk_weff(ec):
            v_weff[ec] = S.emit("vector", lambda g, ec_=ec: g.tensor_add(
                Weff[:, ec_, :], wbig[:, ec_ % 4, :], D2[:]),
                waits=[(f"sB{ec % 4}", c_wT[ec]), ("sD2", c_d2b[0])], inc="sDVE")

        def emit_wdma(idx):
            src3 = wsrc[idx].rearrange("(ec p) o -> p ec o", p=NP)
            half = idx % 2
            if idx == 0:
                ww = [("sDVE", a_apply[11])]
            elif idx == 1:
                ww = [("sPE", p_tp[NTT - 1])]
            else:
                ww = [("sPE", p_wdone[idx - 2])]
            c_w[idx] = dma("sync", wsb[idx % 2][:], src3[:, :, half * 512:(half + 1) * 512],
                           waits=ww, sem=f"sW{idx % 2}")

        for t in range(NTT + 3):
            if t < NTT:
                a_load(t)
            if t < NTT:
                a_stats(t)
                a_sqrt_f(t)
            if 1 <= t <= NTT:
                a_apply_f(t - 1)
            if 3 <= t <= NTT + 2:
                a_copy(t - 3)
            if 2 <= t <= NTT + 1:
                a_transp(t - 2)
            if 2 <= t <= 5:
                mk_wT(t - 2)
            if t == 12:
                dma("sync", D2[0:64, :], disha_d,
                    waits=[("sDVE", a_apply[9])], sem="sD2")
                c_d2b[0] = dma("sync", D2[64:128, :], disha_d, sem="sD2")
                emit_wdma(0)

        c_hs = dma("sync", hsel[:], hsel_d, sem="sCst")

        for ec in range(ECH):
            mk_weff(ec)
            if ec + 4 < ECH:
                mk_wT(ec + 4)

        emit_wdma(1)

        def c_group(mms, evac_q, evac_fn, extra_waits=()):
            nonlocal gi
            par = gi % 4
            nmm = len(mms)
            pw = list(extra_waits)
            if evac_psC[par] is not None:
                pw.append(evac_psC[par])
            pval = None
            for i, mk in enumerate(mms):
                pval = S.emit("tensor", (lambda g, mk_=mk, p=par, first=(i == 0), last=(i == nmm - 1):
                                         mk_(g, psC[p], first, last)),
                              waits=(pw if i == 0 else []),
                              inc=("sPE" if i == nmm - 1 else None))
            ev = S.emit(evac_q, lambda g, p=par: evac_fn(g, psC[p]),
                        waits=[("sPE", pval)], inc=("sACT" if evac_q == "scalar" else "sDVE"))
            evac_psC[par] = (("sACT" if evac_q == "scalar" else "sDVE"), ev)
            gi += 1
            return pval

        for pi, (widx0, bias, scale) in enumerate([(0, bqcs, s125), (2, bkcs, None)]):
            dst = QT if pi == 0 else KT
            for half in range(2):
                widx = widx0 + half
                wpar = widx % 2
                last_p = None
                for tc4 in range(4):
                    for ocl in range(4):
                        oc = half * 4 + ocl
                        mms = []
                        for ec in range(ECH):
                            mms.append(lambda g, ps, first, last, w=wpar, ec_=ec, o_=ocl, t_=tc4:
                                       g.matmul(ps, wsb[w][:, ec_, o_ * NP:(o_ + 1) * NP],
                                                xnT[:, ec_, t_ * 512:(t_ + 1) * 512],
                                                start=first, stop=last))
                        ew = [("sACT", v_xnT[4 * tc4 + 3]), (f"sW{widx % 2}", c_w[widx]),
                              ("sDVE", max(v_bqcs, v_bkcs))]
                        if pi == 1 and half == 1:
                            ew.append(("sDVE", v_weff[ECH - 1]))

                        def ev(g, ps, d=dst, o_=oc, t_=tc4, b=bias, sc=scale):
                            s1 = sc[:, 0:1] if sc is not None else 1.0
                            return g.tensor_scalar(d[:, o_, t_ * 512:(t_ + 1) * 512], ps,
                                                   s1, b[:, o_:o_ + 1], ALU.mult, ALU.add)
                        last_p = c_group(mms, "vector", ev, ew)
                p_wdone[widx] = last_p
                if widx + 2 < 6:
                    emit_wdma(widx + 2)
        v_vevac = [None] * NTT
        for half in range(2):
            widx = 4 + half
            wpar = widx % 2
            last_p = None
            for tt in range(NTT):
                mms = []
                for ec in range(ECH):
                    mms.append(lambda g, ps, first, last, w=wpar, ec_=ec, t_=tt:
                               g.matmul(ps, xnT[:, ec_, t_ * NP:(t_ + 1) * NP],
                                        wsb[w][:, ec_, :], start=first, stop=last))
                mms.append(lambda g, ps, first, last, h=half:
                           g.matmul(ps, ones128[0:1, :], bv_sb[0:1, h * 512:(h + 1) * 512],
                                    start=False, stop=last))
                ew = [("sACT", v_xnT[tt]), (f"sW{widx % 2}", c_w[widx]), ("sCst", c_cst),
                      ("sDVE", v_vones)]

                def ev(g, ps, t_=tt, h=half):
                    pin = ps.rearrange("p (a b) -> p a b", b=64)
                    po = bass.AP(Vp.ap().tensor, Vp.ap().offset + t_ * 1040 + h * 520,
                                 [list(Vp.ap().ap[0]), [65, 8], [1, 64]])
                    return g.tensor_copy(po, pin)
                last_p = c_group(mms, "vector", ev, ew)
                v_vevac[tt] = evac_psC[(gi - 1) % 2][1]
            p_wdone[widx] = last_p
        g_done = {}
        a_gdone = None
        p_gmm = None
        for tt in range(NTT):
            for half in range(2):
                mms = []
                for ec in range(ECH):
                    mms.append(lambda g, ps, first, last, ec_=ec, t_=tt, h=half:
                               g.matmul(ps, xnT[:, ec_, t_ * NP:(t_ + 1) * NP],
                                        Weff[:, ec_, h * 512:(h + 1) * 512],
                                        start=first, stop=last))
                ew = [("sACT", v_xnT[tt]), ("sDVE", v_weff[ECH - 1])]

                def ev(g, ps, t_=tt, h=half):
                    return g.activation(g_sb[:, t_, h * 512:(h + 1) * 512], ps, AF.Silu)
                p_gmm = c_group(mms, "scalar", ev, ew)
                a_gdone = evac_psC[(gi - 1) % 4][1]
                g_done[(tt, half)] = ("sACT", a_gdone)

        c_wo = dma("sync", woT[:], woT_d.rearrange("(ec p) o -> p ec o", p=NP),
                   waits=[("sPE", p_gmm)], sem="sWO")

        dbg_gate = None
        if debug:
            for nm, sb in [("dbg_xnT", xnT), ("dbg_QT", QT), ("dbg_KT", KT),
                           ("dbg_V", Vp)]:
                dbg_gate = dma("sync", dbg[nm], sb[:],
                               waits=[("sACT", a_gdone), ("sDVE", v_vevac[NTT - 1]),
                                      ("sPE", p_gmm)], sem="sDBG")

        pt_last_pv = [None] * 7
        psp_last_exp = [None, None]
        last_evac = [("sACT", v_xnT[NTT - 1])]
        last_evac_gp = [None]
        sh_last = [None]
        tc_norm = [None] * NTC
        norm_meta = [None] * NTC
        n_left = [NPAIR] * NTC
        lnq = []
        gather_gate = [None, None]
        bc_gate = [None, None]
        gp = 0
        u = 0
        pvq = []

        psF = [Q2[:, i, :] for i in range(2)]
        fill_evac = [None, None]
        qt_free = [None] * NTC
        xe_done = [None] * NTC
        xe_ap = {}
        last_out = [None, None]
        FS = dict(fi=0, es=0, credit=0.0, evac=None, out=None, cur=None, cur_kind=None)
        CHUNK = 440.0
        CCAP = 2600.0

        fillq = []
        for tcq in range(NTC):
            for fp in range(NPAIR):
                fillq.append(dict(kind="N", tc=tcq, p=fp, cost=440.0,
                                  ready=(lambda t=tcq: lambda: norm_meta[t] is not None)()))
            for ft in range(4 * tcq, 4 * tcq + 4):
                for fh in range(2):
                    fillq.append(dict(kind="E", tt=ft, h=fh, cost=1925.0,
                                      ready=(lambda t=tcq: lambda: tc_norm[t] is not None)()))

        def mk_n_tile(ent):
            pp, tcq = ent["p"], ent["tc"]
            rv, shv, b_ = norm_meta[tcq]
            tb_ = tcq * 512
            slot = FS["fi"] % 2
            FS["fi"] += 1
            pw = [("sDVE", rv), ("sCst", c_hs)]
            if fill_evac[slot] is not None:
                pw.append(fill_evac[slot])

            def em(sl=slot, pp_=pp, b2=b_, pw_=pw, tcq_=tcq, shv_=shv, t2=tb_):
                pval = S.emit("tensor", lambda g: g.matmul(
                    psF[sl], hsel[0:16, pp_ * NP:(pp_ + 1) * NP],
                    L16[0:16, b2, :],
                    start=True, stop=True), waits=pw_, inc="sPE")
                n_left[tcq_] -= 1
                if pp_ == NPAIR - 1:
                    bc_gate[b2] = ("sPE", pval)

                def evac():
                    mv_ = S.emit("vector", lambda g: g.tensor_tensor(
                        AttnOT[:, pp_, t2:t2 + 512], AttnOT[:, pp_, t2:t2 + 512],
                        psF[sl], ALU.mult), waits=[("sPE", pval), ("sSH0", shv_)],
                        inc="sDVE")
                    fill_evac[sl] = ("sDVE", mv_)
                    if pp_ == NPAIR - 1:
                        tc_norm[tcq_] = (mv_, shv_)
                    return None
                FS["evac"] = evac
            return [em]

        def mk_e_tile(ent):
            tt, oh = ent["tt"], ent["h"]
            tc_ = tt // 4
            if xe_done[tc_] is None:
                for t2 in range(4):
                    for h2 in range(2):
                        ap = QT[:, 2 * t2 + h2, tc_ * 512:(tc_ + 1) * 512]
                        xe_ap[(4 * tc_ + t2, h2)] = ap
                        dma("sync", ap, x_d[(4 * tc_ + t2) * NP:(4 * tc_ + t2 + 1) * NP,
                                            h2 * 512:(h2 + 1) * 512],
                            waits=[("sPE", qt_free[tc_])], sem=f"sX{tc_}")
                xe_done[tc_] = S.cnt[f"sX{tc_}"]
            slot = FS["fi"] % 2
            FS["fi"] += 1
            pw = [("sDVE", tc_norm[tc_][0]), ("sSH0", tc_norm[tc_][1]),
                  ("sWO", c_wo), ("sCst", c_cst)]
            if fill_evac[slot] is not None:
                pw.append(fill_evac[slot])
            ems = []
            for pc in range(ECH + 1):
                def em(pc_=pc, t_=tt, o=oh, sl=slot, tcq=tc_, first=(pc == 0),
                       last=(pc == ECH), pw_=pw):
                    if not last:
                        S.emit("tensor", lambda g: g.matmul(
                            psF[sl], AttnOT[:, pc_, t_ * NP:(t_ + 1) * NP],
                            woT[:, pc_, o * 512:(o + 1) * 512],
                            start=first, stop=False), waits=(pw_ if first else []))
                        return
                    pval = S.emit("tensor", lambda g: g.matmul(
                        psF[sl], ones128[0:1, :], bo_sb[0:1, o * 512:(o + 1) * 512],
                        start=False, stop=True), inc="sPE")

                    def evac():
                        es = FS["es"] % 2
                        FS["es"] += 1
                        wv_ = [("sPE", pval), g_done[(t_, o)]]
                        if last_out[es] is not None:
                            wv_.append((f"sO{es}", last_out[es]))
                        vtm = S.emit("vector", lambda g: g.tensor_tensor(
                            esb[es][:], psF[sl], g_sb[:, t_, o * 512:(o + 1) * 512],
                            ALU.mult), waits=wv_, inc="sDVE")
                        vadd = S.emit("vector", lambda g: g.tensor_tensor(
                            esb[es][:], esb[es][:], xe_ap[(t_, o)], ALU.add),
                            waits=[(f"sX{tcq}", xe_done[tcq])], inc="sDVE")
                        fill_evac[sl] = ("sDVE", vtm)

                        def out():
                            last_out[es] = S.emit("gpsimd", lambda g: g.dma_start(
                                out=out_d[t_ * NP:(t_ + 1) * NP, o * 512:(o + 1) * 512],
                                in_=esb[es][:]), waits=[("sDVE", vadd)], inc=f"sO{es}",
                                amt=16)
                        return out
                    FS["evac"] = evac
                ems.append(em)
            return ems

        def flush_pend():
            if FS["out"] is not None:
                FS["out"]()
                FS["out"] = None
            if FS["evac"] is not None:
                FS["out"] = FS["evac"]()
                FS["evac"] = None

        def pop_chunk():
            if FS["cur"] is None:
                if not fillq:
                    return False
                if not fillq[0]["ready"]():
                    flush_pend()
                    if not fillq[0]["ready"]():
                        return False
                ent = fillq.pop(0)
                flush_pend()
                FS["cur"] = (mk_n_tile if ent["kind"] == "N" else mk_e_tile)(ent)
                FS["cur_kind"] = ent["kind"]
            for _ in range(2):
                if FS["cur"]:
                    FS["cur"].pop(0)()
            if not FS["cur"]:
                FS["cur"] = None
                FS["cur_kind"] = None
            return True

        def psO(lo, hi, h, o0, o1):
            return PS6[lo:hi, 4 + h, o0:o1]

        def d_epilogue(p, tb, p_pv_val):
            nonlocal gp
            tcq = tb // 512
            wr = [("sPE", p_pv_val),
                  ("sSH0", sh_last[0]),
                  ("sLA", S.cnt.get("sLA")), ("sLB", S.cnt.get("sLB"))]
            evA = S.emit("vector", lambda g: g.tensor_copy(
                OddA[0:65, :], PS6[0:65, 4, :]), waits=wr, inc="sDVE")
            evB = S.emit("vector", lambda g: g.tensor_copy(
                OddB[0][0:65, :], PS6[0:65, 5, :]), inc="sDVE")
            last_evac[0] = evB
            wsh = [("sDVE", evB)]
            if dbg_gate is not None:
                wsh.append(("sDBG", dbg_gate))
            S.emit("sync", lambda g, p_=p, t=tb: g.dma_start(
                out=AttnOT[0:64, p_, t:t + 512], in_=OddA[0:64, :]),
                waits=wsh, inc="sSH0", amt=16)
            sh_last[0] = S.emit("sync", lambda g, p_=p, t=tb: g.dma_start(
                out=AttnOT[64:128, p_, t:t + 512], in_=OddB[0][0:64, :]),
                inc="sSH0", amt=16)
            b = (gp // 8) % 2
            semx = "sLA" if b == 0 else "sLB"
            for h in range(2):
                srcb = OddA if h == 0 else OddB[0]
                src = srcb[64:65, :]
                dstp = (2 * (gp % 8) + h)
                dst = bass.AP(L16.ap().tensor, dstp * 2048 + b * 512, [[2048, 1], [1, 512]])
                wg = [("sDVE", evB)]
                if gather_gate[b] is not None:
                    wg.append(gather_gate[b])
                if bc_gate[b] is not None:
                    wg.append(bc_gate[b])
                cg = S.emit("sync", lambda g, d=dst, sp=src: g.dma_start(out=d, in_=sp),
                            waits=wg, inc=semx, amt=16)
            if gp % 8 == 7:
                def mk_aux(b_=b, semx_=semx, tc_=gp // 8, gval=S.cnt[semx]):
                    def go():
                        wt = [(semx_, gval)]
                        if bc_gate[b_] is not None:
                            wt.append(bc_gate[b_])
                        def _recip(g):
                            with nc.allow_low_precision(reason="1/l in f16; error budget analysed"):
                                return g.reciprocal(L16[0:16, b_, :], L16[0:16, b_, :])
                        rv = S.emit("vector", _recip, waits=wt, inc="sDVE")
                        gather_gate[b_] = ("sDVE", rv)
                        norm_meta[tc_] = (rv, sh_last[0], b_)
                    return go
                lnq.append([2, mk_aux()])
            gp += 1

        for tc in range(NTC):
            tb = tc * 512
            nsc = 4 * tc + 4
            if tc >= 2:
                while n_left[tc - 2] > 0:
                    assert pop_chunk()
            for p in range(NPAIR):
                for sc in range(nsc):
                    for ent in lnq:
                        ent[0] -= 1
                    if lnq and lnq[0][0] <= 0:
                        lnq.pop(0)[1]()
                    par = u % 2
                    slot = u % 7
                    off = 0 if sc < 4 * tc else (sc - 4 * tc) * NP
                    N = 512 - off
                    diag = sc >= 4 * tc
                    sb_ = sc * NP
                    w0 = [("sACT", a_gdone)]
                    if u < 4:
                        w0.append(("sACT", v_xnT[NTT - 1]))
                    if psp_last_exp[par] is not None:
                        w0.append(("sACT", psp_last_exp[par]))
                    S.emit("tensor", lambda g, p_=p, par_=par, sb2=sb_, o=off, n=N, t=tb:
                           g.matmul(PS6[:, 2 * par_ + 0, o:o + n], KT[0:64, p_, sb2:sb2 + NP],
                                    QT[0:64, p_, t + o:t + 512], start=True, stop=True),
                           waits=w0)
                    p_sc = S.emit("tensor", lambda g, p_=p, par_=par, sb2=sb_, o=off, n=N, t=tb:
                                  g.matmul(PS6[:, 2 * par_ + 1, o:o + n], KT[64:128, p_, sb2:sb2 + NP],
                                           QT[64:128, p_, t + o:t + 512], start=True, stop=True),
                                  inc="sPE")
                    if p == NPAIR - 1 and sc == nsc - 1:
                        qt_free[tc] = p_sc
                    we = [("sPE", p_sc)]
                    if pt_last_pv[slot] is not None:
                        we.append(("sPE", pt_last_pv[slot]))
                    a_exp = S.emit("scalar", lambda g, par_=par, sl_=slot, o=off, n=N:
                                   g.activation(PT[sl_][:, :, o:o + n],
                                                PS6[:, 2 * par_:2 * par_ + 2, o:o + n],
                                                AF.Exp, bias=neg2[:]), waits=we, inc="sACT")
                    psp_last_exp[par] = a_exp
                    v_tri = None
                    if diag:
                        tb_ap = bass.AP(tri_sb.ap().tensor, tri_sb.ap().offset,
                                        [list(tri_sb.ap().ap[0]), [0, 2], [1, NP]])
                        v_tri = S.emit("gpsimd", lambda g, sl_=slot, o=off, t=tb_ap:
                                       g.tensor_tensor(PT[sl_][:, :, o:o + NP],
                                                       PT[sl_][:, :, o:o + NP],
                                                       t, ALU.mult),
                                       waits=[("sACT", a_exp), ("sCst", c_cst)], inc="sGP")

                    def mk_pv(p_=p, tb_=tb, sl_=slot, o=off, n=N, a_exp_=a_exp, v_tri_=v_tri,
                              vv_=v_vevac[sc], s_=sc,
                              first=(sc == 0), last=(sc == nsc - 1)):
                        def go():
                            wp = [("sACT", a_exp_), ("sDVE", vv_)]
                            if v_tri_ is not None:
                                wp.append(("sGP", v_tri_))
                            if first and last_evac[0] is not None:
                                wp.append(last_evac[0] if isinstance(last_evac[0], tuple)
                                          else ("sDVE", last_evac[0]))
                            if first and last_evac_gp[0] is not None:
                                wp.append(("sGP", last_evac_gp[0]))
                            S.emit("tensor", lambda g: g.matmul(
                                psO(0, 65, 0, o, o + n),
                                Vp[:, s_, (2 * p_) * 65:(2 * p_) * 65 + 65],
                                PT[sl_][:, 0, o:o + n], start=first, stop=last), waits=wp)
                            pv = S.emit("tensor", lambda g: g.matmul(
                                psO(0, 65, 1, o, o + n),
                                Vp[:, s_, (2 * p_ + 1) * 65:(2 * p_ + 1) * 65 + 65],
                                PT[sl_][:, 1, o:o + n], start=first, stop=last), inc="sPE")
                            pt_last_pv[sl_] = pv
                            if last:
                                d_epilogue(p_, tb_, pv)
                        return go
                    pv_e = mk_pv()
                    if len(pvq) >= 5:
                        pvq.pop(0)()
                    pvq.append(pv_e)
                    u += 1
                    FS["credit"] = min(CCAP, FS["credit"]
                                       + (172.0 + 2 * N) / 1.2 - 3.6 * N / 2.4)
                    while FS["credit"] >= CHUNK:
                        if not pop_chunk():
                            break
                        FS["credit"] -= CHUNK
        while pvq:
            pvq.pop(0)()
            while lnq:
                lnq.pop(0)[1]()

        while fillq or FS["cur"] is not None:
            assert pop_chunk(), "filler not ready at drain"
        flush_pend()
        flush_pend()
        for i2 in range(2):
            S.emit("gpsimd", lambda g, i_=i2: g.wait_ge(sems[f"sO{i_}"], last_out[i_]))

        if debug:
            dma("sync", dbg["dbg_AOT"], AttnOT[:],
                waits=[("sDVE", tc_norm[3][0]), ("sSH0", sh_last[0])], sem="sDBG")
            gw = [(s, max((v for s2, v in g_done.values() if s2 == s), default=None))
                  for s in ("sACT", "sDVE")]
            dma("sync", dbg["dbg_g"], g_sb[:], waits=gw, sem="sDBG")
            S.emit("sync", lambda g: g.wait_ge(sems["sDBG"], S.cnt["sDBG"]))

        globals()["_last_cnt"] = dict(S.cnt)
        def run(eng, q):
            lastw = {}
            for fn, waits, inc, amt, _val in S.ops[q]:
                for (s, v) in waits:
                    if lastw.get(s, -1) >= v:
                        continue
                    eng.wait_ge(sems[s], v)
                    lastw[s] = v
                inst = fn(eng)
                if inc is not None:
                    inst.then_inc(sems[inc], amt)

        @block.sync
        def _(eng):
            run(eng, "sync")

        @block.tensor
        def _(eng):
            run(eng, "tensor")

        @block.vector
        def _(eng):
            run(eng, "vector")

        @block.scalar
        def _(eng):
            run(eng, "scalar")

        @block.gpsimd
        def _(eng):
            run(eng, "gpsimd")

    return nc


def _mk_hsel():
    hs = np.zeros((16, E), dtype=np.float16)
    for p in range(NPAIR):
        hs[2 * p, p * NP:p * NP + 64] = 1
        hs[2 * p + 1, p * NP + 64:p * NP + NP] = 1
    return hs


def _pad16(b):
    out = np.zeros((16, NP), dtype=np.float16)
    out[0:ECH] = np.asarray(b).reshape(ECH, NP)
    return out


def make_inputs(inputs):
    f16 = np.float16
    x = np.asarray(inputs["L0_attn_hidden_states"])
    wq = np.asarray(inputs["L0_attn_wq"])
    wk = np.asarray(inputs["L0_attn_wk"])
    wv = np.asarray(inputs["L0_attn_wv"])
    wo = np.asarray(inputs["L0_attn_wo"])
    bq = np.asarray(inputs["L0_attn_bq"])
    bk = np.asarray(inputs["L0_attn_bk"])
    bv = np.asarray(inputs["L0_attn_bv"])
    bo = np.asarray(inputs["L0_attn_bo"])
    w = np.asarray(inputs["L0_ffn_weight"])
    disha = np.asarray(inputs["L0_ffn_disha"])
    common = dict(
        wqT=np.ascontiguousarray(wq.T), wkT=np.ascontiguousarray(wk.T),
        wvT=np.ascontiguousarray(wv.T), woT=np.ascontiguousarray(wo.T),
        wT=np.ascontiguousarray(w.T.astype(np.float32)),
        disha=np.ascontiguousarray(disha.astype(np.float32)),
        bq2=_pad16(bq), bk2=_pad16(bk),
        bv=np.ascontiguousarray(bv.reshape(1, E)),
        bo=np.ascontiguousarray(bo.reshape(1, E)),
        tri=np.triu(np.ones((NP, NP), dtype=f16)),
        eye=np.eye(NP, dtype=np.float32),
        hsel=_mk_hsel(),
    )
    return [dict(x=np.ascontiguousarray(x[b]), **common) for b in range(8)]


def kernel(**inputs):
    in_maps = make_inputs(inputs)
    nc = build_program()
    res = run_bass_kernel_spmd(nc, in_maps, core_ids=list(range(8)))
    return np.stack([res.results[b]["out"] for b in range(8)], axis=0)


# version 52
# speedup vs baseline: 1.1692x; 1.1692x over previous
import numpy as np
import concourse.bass as bass
import concourse.mybir as mybir
from concourse.bass_utils import run_bass_kernel_spmd
from contextlib import ExitStack

F32 = mybir.dt.float32
F16 = mybir.dt.float16
AF = mybir.ActivationFunctionType
ALU = mybir.AluOpType

T, E, H, D = 2048, 1024, 16, 64
NP = 128
NTT = T // NP
ECH = E // NP
NTC = 4
NPAIR = H // 2


class Streams:
    ENGS = ("sync", "tensor", "vector", "scalar", "gpsimd")

    def __init__(self):
        self.ops = {k: [] for k in self.ENGS}
        self.cnt = {}

    def emit(self, q, fn, waits=(), inc=None, amt=1):
        ws = tuple((s, v) for (s, v) in waits if v is not None and v > 0)
        if q == "vector" and inc is None:
            inc = "sDVE"
        if inc is None:
            self.ops[q].append((fn, ws, None, 0, None))
            return None
        self.cnt[inc] = self.cnt.get(inc, 0) + amt
        val = self.cnt[inc]
        if q == "vector":
            prev = getattr(self, "_vprev", None)
            if prev is not None:
                ws = ws + ((prev[0], prev[1]),)
            self._vprev = (inc, val) if inc == "sDVE" else prev
        self.ops[q].append((fn, ws, inc, amt, val))
        return val


def build_program(debug=False):
    nc = bass.Bass("TRN2", target_bir_lowering=False)

    x_d = nc.dram_tensor("x", [T, E], F16, kind="ExternalInput").ap()
    wqT_d = nc.dram_tensor("wqT", [E, E], F16, kind="ExternalInput").ap()
    wkT_d = nc.dram_tensor("wkT", [E, E], F16, kind="ExternalInput").ap()
    wvT_d = nc.dram_tensor("wvT", [E, E], F16, kind="ExternalInput").ap()
    woT_d = nc.dram_tensor("woT", [E, E], F16, kind="ExternalInput").ap()
    wT_d = nc.dram_tensor("wT", [E, E], F32, kind="ExternalInput").ap()
    disha_d = nc.dram_tensor("disha", [64, E], F32, kind="ExternalInput").ap()
    bq_d = nc.dram_tensor("bq2", [16, NP], F16, kind="ExternalInput").ap()
    bk_d = nc.dram_tensor("bk2", [16, NP], F16, kind="ExternalInput").ap()
    bv_d = nc.dram_tensor("bv", [1, E], F16, kind="ExternalInput").ap()
    bo_d = nc.dram_tensor("bo", [1, E], F16, kind="ExternalInput").ap()
    tri_d = nc.dram_tensor("tri", [NP, NP], F16, kind="ExternalInput").ap()
    eye_d = nc.dram_tensor("eye", [NP, NP], F32, kind="ExternalInput").ap()
    hsel_d = nc.dram_tensor("hsel", [16, E], F16, kind="ExternalInput").ap()
    out_d = nc.dram_tensor("out", [T, E], F32, kind="ExternalOutput").ap()
    dbg = {}
    if debug:
        for nm, shp, dt in [("dbg_xnT", [NP, ECH, T], F16), ("dbg_QT", [NP, ECH, T], F16),
                            ("dbg_KT", [NP, ECH, T], F16), ("dbg_V", [NP, NTT, 1040], F16),
                            ("dbg_g", [NP, NTT, E], F16), ("dbg_AOT", [NP, ECH, T], F16)]:
            dbg[nm] = nc.dram_tensor(nm, shp, dt, kind="ExternalOutput").ap()

    base = (nc.sbuf_base + 31) // 32 * 32
    top = nc.sbuf_top

    def at(name, shape, dtype, off):
        per = int(np.prod(shape[1:])) * mybir.dt.size(dtype)
        assert base + off + per <= top, (name, off, per, top - base)
        return nc.alloc_sbuf_tensor_at(name, shape, dtype, offset=base + off)

    o_xnT = 0
    o_QT = 32768
    o_KT = 65536
    o_V = 98304
    o_g = 131584
    o_Wf = 164352
    o_ws = 180736
    o_xn = 188928
    o_xin = 197120
    o_misc = 201216

    xnT = at("xnT", [NP, ECH, T], F16, o_xnT)
    AttnOT = at("AttnOT", [NP, ECH, T], F16, o_xnT)
    QT = at("QT", [NP, ECH, T], F16, o_QT)
    KT = at("KT", [NP, ECH, T], F16, o_KT)
    esb = [at(f"esb{i}", [NP, 512], F32, o_ws + 4096 + 2048 * i) for i in range(2)]
    Vp = at("Vp", [NP, NTT, 1040], F16, o_V)
    g_sb = at("g", [NP, NTT, E], F16, o_g)
    Weff = at("Weff", [NP, ECH, E], F16, o_Wf)
    woT = at("woT", [NP, ECH, E], F16, o_Wf)
    wsb = [at("wsb0", [NP, ECH, 512], F16, o_ws), at("wsb1", [NP, ECH, 512], F16, o_xn)]
    wbuf = [at(f"wbuf{i}", [NP, E], F32, o_ws + 4096 * i) for i in range(2)]
    L16 = at("L16", [NP, 4, 512], F16, o_ws)
    xn_sb = [at(f"xn{i}", [NP, E], F32, o_xn + 4096 * i) for i in range(2)]
    PT = [at(f"PT{i}", [NP, 2, 512], F16, o_xn + 2048 * i) for i in range(4)]
    xin = ([at(f"xin{i}", [NP, E], F16, o_xin + 2048 * i) for i in range(2)]
           + [at(f"xin{i+2}", [NP, E], F16, o_ws + 2048 * i) for i in range(2)]
           + [at(f"xin{i+4}", [NP, E], F16, o_QT + 16384 + 2048 * i) for i in range(4)])
    D2 = at("D2", [NP, E], F32, o_xin)
    wbig = at("wbig", [NP, 4, E], F32, o_KT + 16384)

    mo = o_misc

    def misc(name, shape, dtype):
        nonlocal mo
        h = at(name, shape, dtype, mo)
        per = int(np.prod(shape[1:])) * mybir.dt.size(dtype)
        mo += (per + 31) // 32 * 32
        return h

    tri_sb = misc("tri", [NP, NP], F16)
    eye_sb = misc("eye", [NP, NP], F32)
    ones128 = misc("ones128", [1, NP], F16)
    neg2 = misc("neg2", [NP, 1], F32)
    s125 = misc("s125", [NP, 1], F32)
    bqc = misc("bqc", [NP, 16], F16)
    bkc = misc("bkc", [NP, 16], F16)
    bqcs = misc("bqcs", [NP, ECH], F32)
    bkcs = misc("bkcs", [NP, ECH], F32)
    bv_off = mo
    bv_sb = misc("bv_sb", [1, E], F16)
    bo_sb = misc("bo_sb", [1, E], F16)
    OddB = [at("OddB0", [NP, 512], F16, bv_off)]
    neg1 = misc("neg1", [NP, 1], F32)
    epsc = misc("epsc", [NP, 1], F32)
    st6 = [misc(f"st6_{i}", [NP, 12], F32) for i in range(4)]
    mv = [misc(f"mv_{i}", [NP, 2], F32) for i in range(4)]
    stdt = [misc(f"std_{i}", [NP, 1], F32) for i in range(4)]
    rstd = [misc(f"rstd_{i}", [NP, 1], F32) for i in range(4)]
    nb = [misc(f"nb_{i}", [NP, 1], F32) for i in range(4)]
    hsel = misc("hsel", [16, E], F16)
    OddA = misc("OddA", [NP, 512], F16)
    PT.append(at("PT4", [NP, 2, 512], F16, o_xin))
    PT.append(at("PT5", [NP, 2, 512], F16, o_xin + 2048))
    PT.append(at("PT6", [NP, 2, 512], F16, o_ws + 2048))
    assert base + mo <= top, (mo, top - base)

    S = Streams()
    SEMS = ("sX0", "sX1", "sX2", "sX3", "sW0", "sW1", "sWO", "sCst", "sD2",
            "sSH0", "sSH1", "sO0", "sO1", "sO2", "sO3", "sB0", "sB1", "sB2", "sB3", "sLA",
            "sLB", "sDBG", "sPE", "sACT", "sDVE", "sGP")

    with ExitStack() as ctx:
        e = ctx.enter_context
        PS6 = e(nc.psum_tensor("PS6", [NP, 6, 512], F32))
        Q2 = e(nc.psum_tensor("Q2", [NP, 2, 512], F32))
        for h in (PS6, Q2):
            assert nc.lookup_mloc(h).addr % 2048 == 0, nc.lookup_mloc(h).addr
        psC = [PS6[:, i, :] for i in range(4)]
        psE = psC
        sems = {n: e(nc.semaphore(n)) for n in SEMS}
        block = e(nc.Block())

        def dma(q, out_ap, in_ap, waits=(), transpose=False, sem="sIN"):
            if transpose:
                return S.emit(q, lambda g, o=out_ap, i=in_ap: g.dma_start(out=o, in_=i, transpose=True),
                              waits, sem, 16)
            return S.emit(q, lambda g, o=out_ap, i=in_ap: g.dma_start(out=o, in_=i), waits, sem, 16)

        dma("sync", eye_sb[:], eye_d, sem="sCst")
        dma("sync", tri_sb[:], tri_d, sem="sCst")
        dma("sync", bqc[:], bq_d, transpose=True, sem="sCst")
        dma("sync", bkc[:], bk_d, transpose=True, sem="sCst")
        dma("sync", bv_sb[:], bv_d, sem="sCst")
        c_cst = dma("sync", bo_sb[:], bo_d, sem="sCst")

        S.emit("vector", lambda g: g.memset(ones128[:], 1.0))
        S.emit("vector", lambda g: g.memset(neg2[:], -2.0))
        S.emit("vector", lambda g: g.memset(neg1[:], -1.0))
        S.emit("vector", lambda g: g.memset(epsc[:], 1e-5))
        S.emit("vector", lambda g: g.memset(s125[:], 0.125))
        vones = bass.AP(Vp.ap().tensor, Vp.ap().offset + 64,
                        [list(Vp.ap().ap[0]), [1040, NTT], [65, H], [1, 1]])
        v_vones = S.emit("vector", lambda g: g.memset(vones, 1.0), inc="sDVE")
        v_bqcs = S.emit("vector", lambda g: g.tensor_scalar_mul(bqcs[:], bqc[:, 0:ECH], 0.125),
                        waits=[("sCst", c_cst)], inc="sDVE")
        v_bkcs = S.emit("vector", lambda g: g.tensor_copy(bkcs[:], bkc[:, 0:ECH]),
                        waits=[("sCst", c_cst)], inc="sDVE")

        a_apply = [None] * NTT
        v_xnT = [None] * NTT
        v_stats = [None] * NTT
        c_x = [None] * NTT
        p_tp = [None] * NTT
        a_sqrt = [None] * NTT
        v_nb = [None] * NTT

        a_vr = [None] * NTT
        a_v1 = [None] * NTT

        def a_load(tt):
            par = tt % 8
            c_x[tt] = dma("sync", xin[par][:], x_d[tt * NP:(tt + 1) * NP, :],
                          waits=[("sDVE", a_apply[tt - 8] if tt >= 8 else None),
                                 ("sDVE", a_v1[tt - 4] if tt >= 4 else None)],
                          sem=f"sX{par % 4}")

        def a_stats(tt):
            par = tt % 4
            xp = tt % 8
            S.emit("vector", lambda g, p=par, x=xp: g.bn_stats(st6[p][:, 0:6], xin[x][:, 0:512]),
                   waits=[(f"sX{xp % 4}", c_x[tt])])
            v1 = S.emit("vector", lambda g, p=par, x=xp: g.bn_stats(st6[p][:, 6:12], xin[x][:, 512:1024]),
                        inc="sDVE")
            a_v1[tt] = v1
            v_stats[tt] = S.emit("vector", lambda g, p=par: g.bn_aggr(mv[p][:], st6[p][:]),
                                 waits=[("sDVE", v1),
                                        ("sACT", a_sqrt[tt - 4] if tt >= 4 else None)],
                                 inc="sDVE")

        def a_sqrt_f(tt):
            par = tt % 4
            S.emit("vector", lambda g, p=par: g.tensor_scalar_add(
                stdt[p][:], mv[p][:, 1:2], epsc[:, 0:1]),
                waits=[("sACT", a_sqrt[tt - 4] if tt >= 4 else None)])
            vrc = S.emit("vector", lambda g, p=par: g.reciprocal(stdt[p][:], stdt[p][:]),
                         inc="sDVE")
            a_sqrt[tt] = S.emit("scalar", lambda g, p=par: g.activation(
                rstd[p][:], stdt[p][:], AF.Sqrt),
                waits=[("sDVE", vrc),
                       ("sDVE", a_apply[tt - 4] if tt >= 4 else None)], inc="sACT")

        def a_apply_f(tt):
            par = tt % 2
            a_apply[tt] = S.emit("vector", lambda g, p=par, p4=tt % 4, x=tt % 8:
                                 g.tensor_scalar(xn_sb[p][:], xin[x][:],
                                                 mv[p4][:, 0:1], rstd[p4][:, 0:1],
                                                 ALU.subtract, ALU.mult),
                waits=[("sACT", a_sqrt[tt]),
                       ("sPE", p_tp[tt - 2] if tt >= 2 else None)], inc="sDVE")

        def a_transp(tt):
            for j in range(ECH):
                p_tp[tt] = S.emit("tensor", lambda g, jj=j, p=tt % 2: g.transpose(
                    PS6[:, 4 + jj // 4, (jj % 4) * NP:(jj % 4 + 1) * NP],
                    xn_sb[p][:, jj * NP:(jj + 1) * NP], eye_sb[:]),
                    waits=([("sDVE", a_apply[tt]), ("sCst", c_cst),
                            ("sACT", v_xnT[tt - 1] if tt >= 1 else None)] if j == 0 else []),
                    inc=("sPE" if j == ECH - 1 else None))

        def a_copy(tt):
            src = PS6.ap().rearrange("p a (b c) -> p (a b) c", c=NP)[:, 16:24, :]
            v_xnT[tt] = S.emit("scalar", lambda g, s=src, t=tt: g.copy(
                xnT[:, :, t * NP:(t + 1) * NP], s), waits=[("sPE", p_tp[tt])], inc="sACT")

        v_weff = [None] * ECH
        c_wT = [None] * ECH
        c_d2b = [None]
        wsrc = [wqT_d, wqT_d, wkT_d, wkT_d, wvT_d, wvT_d]
        c_w = [None] * 6
        p_wdone = [None] * 6
        evac_psC = [None, None, None, None]
        gi = 0

        def mk_wT(ec):
            c_wT[ec] = S.emit("sync", lambda g, e=ec: g.dma_start(
                out=wbig[:, e % 4, :], in_=wT_d[e * NP:(e + 1) * NP, :]),
                waits=[("sDVE", v_weff[ec - 4] if ec >= 4 else None)],
                inc=f"sB{ec % 4}", amt=16)

        def mk_weff(ec):
            v_weff[ec] = S.emit("vector", lambda g, ec_=ec: g.tensor_add(
                Weff[:, ec_, :], wbig[:, ec_ % 4, :], D2[:]),
                waits=[(f"sB{ec % 4}", c_wT[ec]), ("sD2", c_d2b[0])], inc="sDVE")

        def emit_wdma(idx):
            src3 = wsrc[idx].rearrange("(ec p) o -> p ec o", p=NP)
            half = idx % 2
            if idx == 0:
                ww = [("sDVE", a_apply[11])]
            elif idx == 1:
                ww = [("sPE", p_tp[NTT - 1])]
            else:
                ww = [("sPE", p_wdone[idx - 2])]
            c_w[idx] = dma("sync", wsb[idx % 2][:], src3[:, :, half * 512:(half + 1) * 512],
                           waits=ww, sem=f"sW{idx % 2}")

        for t in range(NTT + 3):
            if t < NTT:
                a_load(t)
            if t < NTT:
                a_stats(t)
                a_sqrt_f(t)
            if 1 <= t <= NTT:
                a_apply_f(t - 1)
            if 3 <= t <= NTT + 2:
                a_copy(t - 3)
            if 2 <= t <= NTT + 1:
                a_transp(t - 2)
            if 2 <= t <= 5:
                mk_wT(t - 2)
            if t == 12:
                dma("sync", D2[0:64, :], disha_d,
                    waits=[("sDVE", a_apply[9])], sem="sD2")
                c_d2b[0] = dma("sync", D2[64:128, :], disha_d, sem="sD2")
                emit_wdma(0)

        c_hs = dma("sync", hsel[:], hsel_d, sem="sCst")

        for ec in range(ECH):
            mk_weff(ec)
            if ec + 4 < ECH:
                mk_wT(ec + 4)

        emit_wdma(1)

        def c_group(mms, evac_q, evac_fn, extra_waits=()):
            nonlocal gi
            par = gi % 4
            nmm = len(mms)
            pw = list(extra_waits)
            if evac_psC[par] is not None:
                pw.append(evac_psC[par])
            pval = None
            for i, mk in enumerate(mms):
                pval = S.emit("tensor", (lambda g, mk_=mk, p=par, first=(i == 0), last=(i == nmm - 1):
                                         mk_(g, psC[p], first, last)),
                              waits=(pw if i == 0 else []),
                              inc=("sPE" if i == nmm - 1 else None))
            ev = S.emit(evac_q, lambda g, p=par: evac_fn(g, psC[p]),
                        waits=[("sPE", pval)], inc=("sACT" if evac_q == "scalar" else "sDVE"))
            evac_psC[par] = (("sACT" if evac_q == "scalar" else "sDVE"), ev)
            gi += 1
            return pval

        for pi, (widx0, bias, scale) in enumerate([(0, bqcs, s125), (2, bkcs, None)]):
            dst = QT if pi == 0 else KT
            for half in range(2):
                widx = widx0 + half
                wpar = widx % 2
                last_p = None
                for tc4 in range(4):
                    for ocl in range(4):
                        oc = half * 4 + ocl
                        mms = []
                        for ec in range(ECH):
                            mms.append(lambda g, ps, first, last, w=wpar, ec_=ec, o_=ocl, t_=tc4:
                                       g.matmul(ps, wsb[w][:, ec_, o_ * NP:(o_ + 1) * NP],
                                                xnT[:, ec_, t_ * 512:(t_ + 1) * 512],
                                                start=first, stop=last))
                        ew = [("sACT", v_xnT[4 * tc4 + 3]), (f"sW{widx % 2}", c_w[widx]),
                              ("sDVE", max(v_bqcs, v_bkcs))]
                        if pi == 1 and half == 1:
                            ew.append(("sDVE", v_weff[ECH - 1]))

                        def ev(g, ps, d=dst, o_=oc, t_=tc4, b=bias, sc=scale):
                            s1 = sc[:, 0:1] if sc is not None else 1.0
                            return g.tensor_scalar(d[:, o_, t_ * 512:(t_ + 1) * 512], ps,
                                                   s1, b[:, o_:o_ + 1], ALU.mult, ALU.add)
                        last_p = c_group(mms, "vector", ev, ew)
                p_wdone[widx] = last_p
                if widx + 2 < 6:
                    emit_wdma(widx + 2)
        v_vevac = [None] * NTT
        for half in range(2):
            widx = 4 + half
            wpar = widx % 2
            last_p = None
            for tt in range(NTT):
                mms = []
                for ec in range(ECH):
                    mms.append(lambda g, ps, first, last, w=wpar, ec_=ec, t_=tt:
                               g.matmul(ps, xnT[:, ec_, t_ * NP:(t_ + 1) * NP],
                                        wsb[w][:, ec_, :], start=first, stop=last))
                mms.append(lambda g, ps, first, last, h=half:
                           g.matmul(ps, ones128[0:1, :], bv_sb[0:1, h * 512:(h + 1) * 512],
                                    start=False, stop=last))
                ew = [("sACT", v_xnT[tt]), (f"sW{widx % 2}", c_w[widx]), ("sCst", c_cst),
                      ("sDVE", v_vones)]

                def ev(g, ps, t_=tt, h=half):
                    pin = ps.rearrange("p (a b) -> p a b", b=64)
                    po = bass.AP(Vp.ap().tensor, Vp.ap().offset + t_ * 1040 + h * 520,
                                 [list(Vp.ap().ap[0]), [65, 8], [1, 64]])
                    return g.tensor_copy(po, pin)
                last_p = c_group(mms, "vector", ev, ew)
                v_vevac[tt] = evac_psC[(gi - 1) % 2][1]
            p_wdone[widx] = last_p
        g_done = {}
        a_gdone = None
        p_gmm = None
        for tt in range(NTT):
            for half in range(2):
                mms = []
                for ec in range(ECH):
                    mms.append(lambda g, ps, first, last, ec_=ec, t_=tt, h=half:
                               g.matmul(ps, xnT[:, ec_, t_ * NP:(t_ + 1) * NP],
                                        Weff[:, ec_, h * 512:(h + 1) * 512],
                                        start=first, stop=last))
                ew = [("sACT", v_xnT[tt]), ("sDVE", v_weff[ECH - 1])]

                def ev(g, ps, t_=tt, h=half):
                    return g.activation(g_sb[:, t_, h * 512:(h + 1) * 512], ps, AF.Silu)
                p_gmm = c_group(mms, "scalar", ev, ew)
                a_gdone = evac_psC[(gi - 1) % 4][1]
                g_done[(tt, half)] = ("sACT", a_gdone)

        c_wo = dma("sync", woT[:], woT_d.rearrange("(ec p) o -> p ec o", p=NP),
                   waits=[("sPE", p_gmm)], sem="sWO")

        dbg_gate = None
        if debug:
            for nm, sb in [("dbg_xnT", xnT), ("dbg_QT", QT), ("dbg_KT", KT),
                           ("dbg_V", Vp)]:
                dbg_gate = dma("sync", dbg[nm], sb[:],
                               waits=[("sACT", a_gdone), ("sDVE", v_vevac[NTT - 1]),
                                      ("sPE", p_gmm)], sem="sDBG")

        pt_last_pv = [None] * 7
        psp_last_exp = [None, None]
        last_evac = [("sACT", v_xnT[NTT - 1])]
        last_evac_gp = [None]
        sh_last = [None]
        tc_norm = [None] * NTC
        norm_meta = [None] * NTC
        n_left = [NPAIR] * NTC
        lnq = []
        gather_gate = [None, None]
        bc_gate = [None, None]
        gp = 0
        u = 0
        pvq = []

        psF = [Q2[:, i, :] for i in range(2)]
        fill_evac = [None, None]
        qt_free = [None] * NTC
        xe_done = [None] * NTC
        xe_ap = {}
        last_out = [None, None]
        FS = dict(fi=0, es=0, credit=0.0, evac=None, out=None, cur=None, cur_kind=None)
        CHUNK = 440.0
        CCAP = 2600.0

        fillq = []
        for tcq in range(NTC):
            for fp in range(NPAIR):
                fillq.append(dict(kind="N", tc=tcq, p=fp, cost=440.0,
                                  ready=(lambda t=tcq: lambda: norm_meta[t] is not None)()))
            for ft in range(4 * tcq, 4 * tcq + 4):
                for fh in range(2):
                    fillq.append(dict(kind="E", tt=ft, h=fh, cost=1925.0,
                                      ready=(lambda t=tcq: lambda: tc_norm[t] is not None)()))

        def mk_n_tile(ent):
            pp, tcq = ent["p"], ent["tc"]
            rv, shv, b_ = norm_meta[tcq]
            tb_ = tcq * 512
            slot = FS["fi"] % 2
            FS["fi"] += 1
            pw = [("sDVE", rv), ("sCst", c_hs)]
            if fill_evac[slot] is not None:
                pw.append(fill_evac[slot])

            def em(sl=slot, pp_=pp, b2=b_, pw_=pw, tcq_=tcq, shv_=shv, t2=tb_):
                pval = S.emit("tensor", lambda g: g.matmul(
                    psF[sl], hsel[0:16, pp_ * NP:(pp_ + 1) * NP],
                    L16[0:16, b2, :],
                    start=True, stop=True), waits=pw_, inc="sPE")
                n_left[tcq_] -= 1
                if pp_ == NPAIR - 1:
                    bc_gate[b2] = ("sPE", pval)

                def evac():
                    mv_ = S.emit("vector", lambda g: g.tensor_tensor(
                        AttnOT[:, pp_, t2:t2 + 512], AttnOT[:, pp_, t2:t2 + 512],
                        psF[sl], ALU.mult), waits=[("sPE", pval), ("sSH0", shv_)],
                        inc="sDVE")
                    fill_evac[sl] = ("sDVE", mv_)
                    if pp_ == NPAIR - 1:
                        tc_norm[tcq_] = (mv_, shv_)
                    return None
                FS["evac"] = evac
            return [em]

        def mk_e_tile(ent):
            tt, oh = ent["tt"], ent["h"]
            tc_ = tt // 4
            if xe_done[tc_] is None:
                for t2 in range(4):
                    for h2 in range(2):
                        ap = QT[:, 2 * t2 + h2, tc_ * 512:(tc_ + 1) * 512]
                        xe_ap[(4 * tc_ + t2, h2)] = ap
                        dma("sync", ap, x_d[(4 * tc_ + t2) * NP:(4 * tc_ + t2 + 1) * NP,
                                            h2 * 512:(h2 + 1) * 512],
                            waits=[("sPE", qt_free[tc_])], sem=f"sX{tc_}")
                xe_done[tc_] = S.cnt[f"sX{tc_}"]
            slot = FS["fi"] % 2
            FS["fi"] += 1
            pw = [("sDVE", tc_norm[tc_][0]), ("sSH0", tc_norm[tc_][1]),
                  ("sWO", c_wo), ("sCst", c_cst)]
            if fill_evac[slot] is not None:
                pw.append(fill_evac[slot])
            ems = []
            for pc in range(ECH + 1):
                def em(pc_=pc, t_=tt, o=oh, sl=slot, tcq=tc_, first=(pc == 0),
                       last=(pc == ECH), pw_=pw):
                    if not last:
                        S.emit("tensor", lambda g: g.matmul(
                            psF[sl], AttnOT[:, pc_, t_ * NP:(t_ + 1) * NP],
                            woT[:, pc_, o * 512:(o + 1) * 512],
                            start=first, stop=False), waits=(pw_ if first else []))
                        return
                    pval = S.emit("tensor", lambda g: g.matmul(
                        psF[sl], ones128[0:1, :], bo_sb[0:1, o * 512:(o + 1) * 512],
                        start=False, stop=True), inc="sPE")

                    def evac():
                        es = FS["es"] % 2
                        FS["es"] += 1
                        wv_ = [("sPE", pval), g_done[(t_, o)]]
                        if last_out[es] is not None:
                            wv_.append((f"sO{es}", last_out[es]))
                        vtm = S.emit("vector", lambda g: g.tensor_tensor(
                            esb[es][:], psF[sl], g_sb[:, t_, o * 512:(o + 1) * 512],
                            ALU.mult), waits=wv_, inc="sDVE")
                        vadd = S.emit("vector", lambda g: g.tensor_tensor(
                            esb[es][:], esb[es][:], xe_ap[(t_, o)], ALU.add),
                            waits=[(f"sX{tcq}", xe_done[tcq])], inc="sDVE")
                        fill_evac[sl] = ("sDVE", vtm)

                        def out():
                            last_out[es] = S.emit("gpsimd", lambda g: g.dma_start(
                                out=out_d[t_ * NP:(t_ + 1) * NP, o * 512:(o + 1) * 512],
                                in_=esb[es][:]), waits=[("sDVE", vadd)], inc=f"sO{es}",
                                amt=16)
                        return out
                    FS["evac"] = evac
                ems.append(em)
            return ems

        def flush_pend():
            if FS["out"] is not None:
                FS["out"]()
                FS["out"] = None
            if FS["evac"] is not None:
                FS["out"] = FS["evac"]()
                FS["evac"] = None

        def pop_chunk():
            if FS["cur"] is None:
                if not fillq:
                    return False
                if not fillq[0]["ready"]():
                    flush_pend()
                    if not fillq[0]["ready"]():
                        return False
                ent = fillq.pop(0)
                flush_pend()
                FS["cur"] = (mk_n_tile if ent["kind"] == "N" else mk_e_tile)(ent)
                FS["cur_kind"] = ent["kind"]
            for _ in range(2):
                if FS["cur"]:
                    FS["cur"].pop(0)()
            if not FS["cur"]:
                FS["cur"] = None
                FS["cur_kind"] = None
            return True

        def psO(lo, hi, h, o0, o1):
            return PS6[lo:hi, 4 + h, o0:o1]

        def d_epilogue(p, tb, p_pv_val):
            nonlocal gp
            tcq = tb // 512
            wr = [("sPE", p_pv_val),
                  ("sSH0", sh_last[0]),
                  ("sLA", S.cnt.get("sLA")), ("sLB", S.cnt.get("sLB"))]
            evA = S.emit("vector", lambda g: g.tensor_copy(
                OddA[0:65, :], PS6[0:65, 4, :]), waits=wr, inc="sDVE")
            evB = S.emit("vector", lambda g: g.tensor_copy(
                OddB[0][0:65, :], PS6[0:65, 5, :]), inc="sDVE")
            last_evac[0] = evB
            wsh = [("sDVE", evB)]
            if dbg_gate is not None:
                wsh.append(("sDBG", dbg_gate))
            S.emit("sync", lambda g, p_=p, t=tb: g.dma_start(
                out=AttnOT[0:64, p_, t:t + 512], in_=OddA[0:64, :]),
                waits=wsh, inc="sSH0", amt=16)
            sh_last[0] = S.emit("sync", lambda g, p_=p, t=tb: g.dma_start(
                out=AttnOT[64:128, p_, t:t + 512], in_=OddB[0][0:64, :]),
                inc="sSH0", amt=16)
            b = (gp // 8) % 2
            semx = "sLA" if b == 0 else "sLB"
            for h in range(2):
                srcb = OddA if h == 0 else OddB[0]
                src = srcb[64:65, :]
                dstp = (2 * (gp % 8) + h)
                dst = bass.AP(L16.ap().tensor, dstp * 2048 + b * 512, [[2048, 1], [1, 512]])
                wg = [("sDVE", evB)]
                if gather_gate[b] is not None:
                    wg.append(gather_gate[b])
                if bc_gate[b] is not None:
                    wg.append(bc_gate[b])
                cg = S.emit("sync", lambda g, d=dst, sp=src: g.dma_start(out=d, in_=sp),
                            waits=wg, inc=semx, amt=16)
            if gp % 8 == 7:
                def mk_aux(b_=b, semx_=semx, tc_=gp // 8, gval=S.cnt[semx]):
                    def go():
                        wt = [(semx_, gval)]
                        if bc_gate[b_] is not None:
                            wt.append(bc_gate[b_])
                        def _recip(g):
                            with nc.allow_low_precision(reason="1/l in f16; error budget analysed"):
                                return g.reciprocal(L16[0:16, b_, :], L16[0:16, b_, :])
                        rv = S.emit("vector", _recip, waits=wt, inc="sDVE")
                        gather_gate[b_] = ("sDVE", rv)
                        norm_meta[tc_] = (rv, sh_last[0], b_)
                    return go
                lnq.append([2, mk_aux()])
            gp += 1

        for tc in range(NTC):
            tb = tc * 512
            nsc = 4 * tc + 4
            if tc >= 2:
                while n_left[tc - 2] > 0:
                    assert pop_chunk()
            for p in range(NPAIR):
                for sc in range(nsc):
                    for ent in lnq:
                        ent[0] -= 1
                    if lnq and lnq[0][0] <= 0:
                        lnq.pop(0)[1]()
                    par = u % 2
                    slot = u % 7
                    off = 0 if sc < 4 * tc else (sc - 4 * tc) * NP
                    N = 512 - off
                    diag = sc >= 4 * tc
                    sb_ = sc * NP
                    w0 = [("sACT", a_gdone)]
                    if u < 4:
                        w0.append(("sACT", v_xnT[NTT - 1]))
                    if psp_last_exp[par] is not None:
                        w0.append(("sACT", psp_last_exp[par]))
                    S.emit("tensor", lambda g, p_=p, par_=par, sb2=sb_, o=off, n=N, t=tb:
                           g.matmul(PS6[:, 2 * par_ + 0, o:o + n], KT[0:64, p_, sb2:sb2 + NP],
                                    QT[0:64, p_, t + o:t + 512], start=True, stop=True),
                           waits=w0)
                    p_sc = S.emit("tensor", lambda g, p_=p, par_=par, sb2=sb_, o=off, n=N, t=tb:
                                  g.matmul(PS6[:, 2 * par_ + 1, o:o + n], KT[64:128, p_, sb2:sb2 + NP],
                                           QT[64:128, p_, t + o:t + 512], start=True, stop=True),
                                  inc="sPE")
                    if p == NPAIR - 1 and sc == nsc - 1:
                        qt_free[tc] = p_sc
                    we = [("sPE", p_sc)]
                    if pt_last_pv[slot] is not None:
                        we.append(("sPE", pt_last_pv[slot]))
                    a_exp = S.emit("scalar", lambda g, par_=par, sl_=slot, o=off, n=N:
                                   g.activation(PT[sl_][:, :, o:o + n],
                                                PS6[:, 2 * par_:2 * par_ + 2, o:o + n],
                                                AF.Exp, bias=neg2[:]), waits=we, inc="sACT")
                    psp_last_exp[par] = a_exp
                    v_tri = None
                    if diag:
                        tb_ap = bass.AP(tri_sb.ap().tensor, tri_sb.ap().offset,
                                        [list(tri_sb.ap().ap[0]), [0, 2], [1, NP]])
                        v_tri = S.emit("gpsimd", lambda g, sl_=slot, o=off, t=tb_ap:
                                       g.tensor_tensor(PT[sl_][:, :, o:o + NP],
                                                       PT[sl_][:, :, o:o + NP],
                                                       t, ALU.mult),
                                       waits=[("sACT", a_exp), ("sCst", c_cst)], inc="sGP")

                    def mk_pv(p_=p, tb_=tb, sl_=slot, o=off, n=N, a_exp_=a_exp, v_tri_=v_tri,
                              vv_=v_vevac[sc], s_=sc,
                              first=(sc == 0), last=(sc == nsc - 1)):
                        def go():
                            wp = [("sACT", a_exp_), ("sDVE", vv_)]
                            if v_tri_ is not None:
                                wp.append(("sGP", v_tri_))
                            if first and last_evac[0] is not None:
                                wp.append(last_evac[0] if isinstance(last_evac[0], tuple)
                                          else ("sDVE", last_evac[0]))
                            if first and last_evac_gp[0] is not None:
                                wp.append(("sGP", last_evac_gp[0]))
                            S.emit("tensor", lambda g: g.matmul(
                                psO(0, 65, 0, o, o + n),
                                Vp[:, s_, (2 * p_) * 65:(2 * p_) * 65 + 65],
                                PT[sl_][:, 0, o:o + n], start=first, stop=last), waits=wp)
                            pv = S.emit("tensor", lambda g: g.matmul(
                                psO(0, 65, 1, o, o + n),
                                Vp[:, s_, (2 * p_ + 1) * 65:(2 * p_ + 1) * 65 + 65],
                                PT[sl_][:, 1, o:o + n], start=first, stop=last), inc="sPE")
                            pt_last_pv[sl_] = pv
                            if last:
                                d_epilogue(p_, tb_, pv)
                        return go
                    pv_e = mk_pv()
                    if len(pvq) >= 5:
                        pvq.pop(0)()
                    pvq.append(pv_e)
                    u += 1
                    FS["credit"] = min(CCAP, FS["credit"]
                                       + (172.0 + 2 * N) / 1.2 - 4.0 * N / 2.4)
                    while FS["credit"] >= CHUNK:
                        if not pop_chunk():
                            break
                        FS["credit"] -= CHUNK
        while pvq:
            pvq.pop(0)()
            while lnq:
                lnq.pop(0)[1]()

        while fillq or FS["cur"] is not None:
            assert pop_chunk(), "filler not ready at drain"
        flush_pend()
        flush_pend()
        for i2 in range(2):
            S.emit("gpsimd", lambda g, i_=i2: g.wait_ge(sems[f"sO{i_}"], last_out[i_]))

        if debug:
            dma("sync", dbg["dbg_AOT"], AttnOT[:],
                waits=[("sDVE", tc_norm[3][0]), ("sSH0", sh_last[0])], sem="sDBG")
            gw = [(s, max((v for s2, v in g_done.values() if s2 == s), default=None))
                  for s in ("sACT", "sDVE")]
            dma("sync", dbg["dbg_g"], g_sb[:], waits=gw, sem="sDBG")
            S.emit("sync", lambda g: g.wait_ge(sems["sDBG"], S.cnt["sDBG"]))

        globals()["_last_cnt"] = dict(S.cnt)
        def run(eng, q):
            lastw = {}
            for fn, waits, inc, amt, _val in S.ops[q]:
                for (s, v) in waits:
                    if lastw.get(s, -1) >= v:
                        continue
                    eng.wait_ge(sems[s], v)
                    lastw[s] = v
                inst = fn(eng)
                if inc is not None:
                    inst.then_inc(sems[inc], amt)

        @block.sync
        def _(eng):
            run(eng, "sync")

        @block.tensor
        def _(eng):
            run(eng, "tensor")

        @block.vector
        def _(eng):
            run(eng, "vector")

        @block.scalar
        def _(eng):
            run(eng, "scalar")

        @block.gpsimd
        def _(eng):
            run(eng, "gpsimd")

    return nc


def _mk_hsel():
    hs = np.zeros((16, E), dtype=np.float16)
    for p in range(NPAIR):
        hs[2 * p, p * NP:p * NP + 64] = 1
        hs[2 * p + 1, p * NP + 64:p * NP + NP] = 1
    return hs


def _pad16(b):
    out = np.zeros((16, NP), dtype=np.float16)
    out[0:ECH] = np.asarray(b).reshape(ECH, NP)
    return out


def make_inputs(inputs):
    f16 = np.float16
    x = np.asarray(inputs["L0_attn_hidden_states"])
    wq = np.asarray(inputs["L0_attn_wq"])
    wk = np.asarray(inputs["L0_attn_wk"])
    wv = np.asarray(inputs["L0_attn_wv"])
    wo = np.asarray(inputs["L0_attn_wo"])
    bq = np.asarray(inputs["L0_attn_bq"])
    bk = np.asarray(inputs["L0_attn_bk"])
    bv = np.asarray(inputs["L0_attn_bv"])
    bo = np.asarray(inputs["L0_attn_bo"])
    w = np.asarray(inputs["L0_ffn_weight"])
    disha = np.asarray(inputs["L0_ffn_disha"])
    common = dict(
        wqT=np.ascontiguousarray(wq.T), wkT=np.ascontiguousarray(wk.T),
        wvT=np.ascontiguousarray(wv.T), woT=np.ascontiguousarray(wo.T),
        wT=np.ascontiguousarray(w.T.astype(np.float32)),
        disha=np.ascontiguousarray(disha.astype(np.float32)),
        bq2=_pad16(bq), bk2=_pad16(bk),
        bv=np.ascontiguousarray(bv.reshape(1, E)),
        bo=np.ascontiguousarray(bo.reshape(1, E)),
        tri=np.triu(np.ones((NP, NP), dtype=f16)),
        eye=np.eye(NP, dtype=np.float32),
        hsel=_mk_hsel(),
    )
    return [dict(x=np.ascontiguousarray(x[b]), **common) for b in range(8)]


def kernel(**inputs):
    in_maps = make_inputs(inputs)
    nc = build_program()
    res = run_bass_kernel_spmd(nc, in_maps, core_ids=list(range(8)))
    return np.stack([res.results[b]["out"] for b in range(8)], axis=0)


# version 53 (final)
# speedup vs baseline: 1.1827x; 1.0115x over previous
import numpy as np
import concourse.bass as bass
import concourse.mybir as mybir
from concourse.bass_utils import run_bass_kernel_spmd
from contextlib import ExitStack

F32 = mybir.dt.float32
F16 = mybir.dt.float16
AF = mybir.ActivationFunctionType
ALU = mybir.AluOpType

T, E, H, D = 2048, 1024, 16, 64
NP = 128
NTT = T // NP
ECH = E // NP
NTC = 4
NPAIR = H // 2


class Streams:
    ENGS = ("sync", "tensor", "vector", "scalar", "gpsimd")

    def __init__(self):
        self.ops = {k: [] for k in self.ENGS}
        self.cnt = {}

    def emit(self, q, fn, waits=(), inc=None, amt=1):
        ws = tuple((s, v) for (s, v) in waits if v is not None and v > 0)
        if q == "vector" and inc is None:
            inc = "sDVE"
        if inc is None:
            self.ops[q].append((fn, ws, None, 0, None))
            return None
        self.cnt[inc] = self.cnt.get(inc, 0) + amt
        val = self.cnt[inc]
        if q == "vector":
            prev = getattr(self, "_vprev", None)
            if prev is not None:
                ws = ws + ((prev[0], prev[1]),)
            self._vprev = (inc, val) if inc == "sDVE" else prev
        self.ops[q].append((fn, ws, inc, amt, val))
        return val


def build_program(debug=False):
    nc = bass.Bass("TRN2", target_bir_lowering=False)

    x_d = nc.dram_tensor("x", [T, E], F16, kind="ExternalInput").ap()
    wqT_d = nc.dram_tensor("wqT", [E, E], F16, kind="ExternalInput").ap()
    wkT_d = nc.dram_tensor("wkT", [E, E], F16, kind="ExternalInput").ap()
    wvT_d = nc.dram_tensor("wvT", [E, E], F16, kind="ExternalInput").ap()
    woT_d = nc.dram_tensor("woT", [E, E], F16, kind="ExternalInput").ap()
    wT_d = nc.dram_tensor("wT", [E, E], F32, kind="ExternalInput").ap()
    disha_d = nc.dram_tensor("disha", [64, E], F32, kind="ExternalInput").ap()
    bq_d = nc.dram_tensor("bq2", [16, NP], F16, kind="ExternalInput").ap()
    bk_d = nc.dram_tensor("bk2", [16, NP], F16, kind="ExternalInput").ap()
    bv_d = nc.dram_tensor("bv", [1, E], F16, kind="ExternalInput").ap()
    bo_d = nc.dram_tensor("bo", [1, E], F16, kind="ExternalInput").ap()
    tri_d = nc.dram_tensor("tri", [NP, NP], F16, kind="ExternalInput").ap()
    eye_d = nc.dram_tensor("eye", [NP, NP], F32, kind="ExternalInput").ap()
    hsel_d = nc.dram_tensor("hsel", [16, E], F16, kind="ExternalInput").ap()
    out_d = nc.dram_tensor("out", [T, E], F32, kind="ExternalOutput").ap()
    dbg = {}
    if debug:
        for nm, shp, dt in [("dbg_xnT", [NP, ECH, T], F16), ("dbg_QT", [NP, ECH, T], F16),
                            ("dbg_KT", [NP, ECH, T], F16), ("dbg_V", [NP, NTT, 1040], F16),
                            ("dbg_g", [NP, NTT, E], F16), ("dbg_AOT", [NP, ECH, T], F16)]:
            dbg[nm] = nc.dram_tensor(nm, shp, dt, kind="ExternalOutput").ap()

    base = (nc.sbuf_base + 31) // 32 * 32
    top = nc.sbuf_top

    def at(name, shape, dtype, off):
        per = int(np.prod(shape[1:])) * mybir.dt.size(dtype)
        assert base + off + per <= top, (name, off, per, top - base)
        return nc.alloc_sbuf_tensor_at(name, shape, dtype, offset=base + off)

    o_xnT = 0
    o_QT = 32768
    o_KT = 65536
    o_V = 98304
    o_g = 131584
    o_Wf = 164352
    o_ws = 180736
    o_xn = 188928
    o_xin = 197120
    o_misc = 201216

    xnT = at("xnT", [NP, ECH, T], F16, o_xnT)
    AttnOT = at("AttnOT", [NP, ECH, T], F16, o_xnT)
    QT = at("QT", [NP, ECH, T], F16, o_QT)
    KT = at("KT", [NP, ECH, T], F16, o_KT)
    esb = [at(f"esb{i}", [NP, 512], F32, o_ws + 4096 + 2048 * i) for i in range(2)]
    Vp = at("Vp", [NP, NTT, 1040], F16, o_V)
    g_sb = at("g", [NP, NTT, E], F16, o_g)
    Weff = at("Weff", [NP, ECH, E], F16, o_Wf)
    woT = at("woT", [NP, ECH, E], F16, o_Wf)
    wsb = [at("wsb0", [NP, ECH, 512], F16, o_ws), at("wsb1", [NP, ECH, 512], F16, o_xn)]
    wbuf = [at(f"wbuf{i}", [NP, E], F32, o_ws + 4096 * i) for i in range(2)]
    L16 = at("L16", [NP, 4, 512], F16, o_ws)
    xn_sb = [at(f"xn{i}", [NP, E], F32, o_xn + 4096 * i) for i in range(2)]
    PT = [at(f"PT{i}", [NP, 2, 512], F16, o_xn + 2048 * i) for i in range(4)]
    xin = ([at(f"xin{i}", [NP, E], F16, o_xin + 2048 * i) for i in range(2)]
           + [at(f"xin{i+2}", [NP, E], F16, o_ws + 2048 * i) for i in range(2)]
           + [at(f"xin{i+4}", [NP, E], F16, o_QT + 16384 + 2048 * i) for i in range(4)])
    D2 = at("D2", [NP, E], F32, o_xin)
    wbig = at("wbig", [NP, 4, E], F32, o_KT + 16384)

    mo = o_misc

    def misc(name, shape, dtype):
        nonlocal mo
        h = at(name, shape, dtype, mo)
        per = int(np.prod(shape[1:])) * mybir.dt.size(dtype)
        mo += (per + 31) // 32 * 32
        return h

    tri_sb = misc("tri", [NP, NP], F16)
    eye_sb = misc("eye", [NP, NP], F32)
    ones128 = misc("ones128", [1, NP], F16)
    neg2 = misc("neg2", [NP, 1], F32)
    s125 = misc("s125", [NP, 1], F32)
    bqc = misc("bqc", [NP, 16], F16)
    bkc = misc("bkc", [NP, 16], F16)
    bqcs = misc("bqcs", [NP, ECH], F32)
    bkcs = misc("bkcs", [NP, ECH], F32)
    bv_off = mo
    bv_sb = misc("bv_sb", [1, E], F16)
    bo_sb = misc("bo_sb", [1, E], F16)
    OddB = [at("OddB0", [NP, 512], F16, bv_off)]
    neg1 = misc("neg1", [NP, 1], F32)
    epsc = misc("epsc", [NP, 1], F32)
    st6 = [misc(f"st6_{i}", [NP, 12], F32) for i in range(4)]
    mv = [misc(f"mv_{i}", [NP, 2], F32) for i in range(4)]
    stdt = [misc(f"std_{i}", [NP, 1], F32) for i in range(4)]
    rstd = [misc(f"rstd_{i}", [NP, 1], F32) for i in range(4)]
    nb = [misc(f"nb_{i}", [NP, 1], F32) for i in range(4)]
    hsel = misc("hsel", [16, E], F16)
    OddA = misc("OddA", [NP, 512], F16)
    PT.append(at("PT4", [NP, 2, 512], F16, o_xin))
    PT.append(at("PT5", [NP, 2, 512], F16, o_xin + 2048))
    PT.append(at("PT6", [NP, 2, 512], F16, o_ws + 2048))
    assert base + mo <= top, (mo, top - base)

    S = Streams()
    SEMS = ("sX0", "sX1", "sX2", "sX3", "sW0", "sW1", "sWO", "sCst", "sD2",
            "sSH0", "sSH1", "sO0", "sO1", "sO2", "sO3", "sB0", "sB1", "sB2", "sB3", "sLA",
            "sLB", "sDBG", "sPE", "sACT", "sDVE", "sGP")

    with ExitStack() as ctx:
        e = ctx.enter_context
        PS6 = e(nc.psum_tensor("PS6", [NP, 6, 512], F32))
        Q2 = e(nc.psum_tensor("Q2", [NP, 2, 512], F32))
        for h in (PS6, Q2):
            assert nc.lookup_mloc(h).addr % 2048 == 0, nc.lookup_mloc(h).addr
        psC = [PS6[:, i, :] for i in range(4)]
        psE = psC
        sems = {n: e(nc.semaphore(n)) for n in SEMS}
        block = e(nc.Block())

        def dma(q, out_ap, in_ap, waits=(), transpose=False, sem="sIN"):
            if transpose:
                return S.emit(q, lambda g, o=out_ap, i=in_ap: g.dma_start(out=o, in_=i, transpose=True),
                              waits, sem, 16)
            return S.emit(q, lambda g, o=out_ap, i=in_ap: g.dma_start(out=o, in_=i), waits, sem, 16)

        dma("sync", eye_sb[:], eye_d, sem="sCst")
        dma("sync", tri_sb[:], tri_d, sem="sCst")
        dma("sync", bqc[:], bq_d, transpose=True, sem="sCst")
        dma("sync", bkc[:], bk_d, transpose=True, sem="sCst")
        dma("sync", bv_sb[:], bv_d, sem="sCst")
        c_cst = dma("sync", bo_sb[:], bo_d, sem="sCst")

        S.emit("vector", lambda g: g.memset(ones128[:], 1.0))
        S.emit("vector", lambda g: g.memset(neg2[:], -2.0))
        S.emit("vector", lambda g: g.memset(neg1[:], -1.0))
        S.emit("vector", lambda g: g.memset(epsc[:], 1e-5))
        S.emit("vector", lambda g: g.memset(s125[:], 0.125))
        vones = bass.AP(Vp.ap().tensor, Vp.ap().offset + 64,
                        [list(Vp.ap().ap[0]), [1040, NTT], [65, H], [1, 1]])
        v_vones = S.emit("vector", lambda g: g.memset(vones, 1.0), inc="sDVE")
        v_bqcs = S.emit("vector", lambda g: g.tensor_scalar_mul(bqcs[:], bqc[:, 0:ECH], 0.125),
                        waits=[("sCst", c_cst)], inc="sDVE")
        v_bkcs = S.emit("vector", lambda g: g.tensor_copy(bkcs[:], bkc[:, 0:ECH]),
                        waits=[("sCst", c_cst)], inc="sDVE")

        a_apply = [None] * NTT
        v_xnT = [None] * NTT
        v_stats = [None] * NTT
        c_x = [None] * NTT
        p_tp = [None] * NTT
        a_sqrt = [None] * NTT
        v_nb = [None] * NTT

        a_vr = [None] * NTT
        a_v1 = [None] * NTT

        def a_load(tt):
            par = tt % 8
            c_x[tt] = dma("sync", xin[par][:], x_d[tt * NP:(tt + 1) * NP, :],
                          waits=[("sDVE", a_apply[tt - 8] if tt >= 8 else None),
                                 ("sDVE", a_v1[tt - 4] if tt >= 4 else None)],
                          sem=f"sX{par % 4}")

        def a_stats(tt):
            par = tt % 4
            xp = tt % 8
            S.emit("vector", lambda g, p=par, x=xp: g.bn_stats(st6[p][:, 0:6], xin[x][:, 0:512]),
                   waits=[(f"sX{xp % 4}", c_x[tt])])
            v1 = S.emit("vector", lambda g, p=par, x=xp: g.bn_stats(st6[p][:, 6:12], xin[x][:, 512:1024]),
                        inc="sDVE")
            a_v1[tt] = v1
            v_stats[tt] = S.emit("vector", lambda g, p=par: g.bn_aggr(mv[p][:], st6[p][:]),
                                 waits=[("sDVE", v1),
                                        ("sACT", a_sqrt[tt - 4] if tt >= 4 else None)],
                                 inc="sDVE")

        def a_sqrt_f(tt):
            par = tt % 4
            S.emit("vector", lambda g, p=par: g.tensor_scalar_add(
                stdt[p][:], mv[p][:, 1:2], epsc[:, 0:1]),
                waits=[("sACT", a_sqrt[tt - 4] if tt >= 4 else None)])
            vrc = S.emit("vector", lambda g, p=par: g.reciprocal(stdt[p][:], stdt[p][:]),
                         inc="sDVE")
            a_sqrt[tt] = S.emit("scalar", lambda g, p=par: g.activation(
                rstd[p][:], stdt[p][:], AF.Sqrt),
                waits=[("sDVE", vrc),
                       ("sDVE", a_apply[tt - 4] if tt >= 4 else None)], inc="sACT")

        def a_apply_f(tt):
            par = tt % 2
            a_apply[tt] = S.emit("vector", lambda g, p=par, p4=tt % 4, x=tt % 8:
                                 g.tensor_scalar(xn_sb[p][:], xin[x][:],
                                                 mv[p4][:, 0:1], rstd[p4][:, 0:1],
                                                 ALU.subtract, ALU.mult),
                waits=[("sACT", a_sqrt[tt]),
                       ("sPE", p_tp[tt - 2] if tt >= 2 else None)], inc="sDVE")

        def a_transp(tt):
            for j in range(ECH):
                p_tp[tt] = S.emit("tensor", lambda g, jj=j, p=tt % 2: g.transpose(
                    PS6[:, 4 + jj // 4, (jj % 4) * NP:(jj % 4 + 1) * NP],
                    xn_sb[p][:, jj * NP:(jj + 1) * NP], eye_sb[:]),
                    waits=([("sDVE", a_apply[tt]), ("sCst", c_cst),
                            ("sACT", v_xnT[tt - 1] if tt >= 1 else None)] if j == 0 else []),
                    inc=("sPE" if j == ECH - 1 else None))

        def a_copy(tt):
            src = PS6.ap().rearrange("p a (b c) -> p (a b) c", c=NP)[:, 16:24, :]
            v_xnT[tt] = S.emit("scalar", lambda g, s=src, t=tt: g.copy(
                xnT[:, :, t * NP:(t + 1) * NP], s), waits=[("sPE", p_tp[tt])], inc="sACT")

        v_weff = [None] * ECH
        c_wT = [None] * ECH
        c_d2b = [None]
        wsrc = [wqT_d, wqT_d, wkT_d, wkT_d, wvT_d, wvT_d]
        c_w = [None] * 6
        p_wdone = [None] * 6
        evac_psC = [None, None, None, None]
        gi = 0

        def mk_wT(ec):
            c_wT[ec] = S.emit("sync", lambda g, e=ec: g.dma_start(
                out=wbig[:, e % 4, :], in_=wT_d[e * NP:(e + 1) * NP, :]),
                waits=[("sDVE", v_weff[ec - 4] if ec >= 4 else None)],
                inc=f"sB{ec % 4}", amt=16)

        def mk_weff(ec):
            v_weff[ec] = S.emit("vector", lambda g, ec_=ec: g.tensor_add(
                Weff[:, ec_, :], wbig[:, ec_ % 4, :], D2[:]),
                waits=[(f"sB{ec % 4}", c_wT[ec]), ("sD2", c_d2b[0])], inc="sDVE")

        def emit_wdma(idx):
            src3 = wsrc[idx].rearrange("(ec p) o -> p ec o", p=NP)
            half = idx % 2
            if idx == 0:
                ww = [("sDVE", a_apply[11])]
            elif idx == 1:
                ww = [("sPE", p_tp[NTT - 1])]
            else:
                ww = [("sPE", p_wdone[idx - 2])]
            c_w[idx] = dma("sync", wsb[idx % 2][:], src3[:, :, half * 512:(half + 1) * 512],
                           waits=ww, sem=f"sW{idx % 2}")

        for t in range(NTT + 3):
            if t < NTT:
                a_load(t)
            if t < NTT:
                a_stats(t)
                a_sqrt_f(t)
            if 1 <= t <= NTT:
                a_apply_f(t - 1)
            if 3 <= t <= NTT + 2:
                a_copy(t - 3)
            if 2 <= t <= NTT + 1:
                a_transp(t - 2)
            if 2 <= t <= 5:
                mk_wT(t - 2)
            if t == 12:
                dma("sync", D2[0:64, :], disha_d,
                    waits=[("sDVE", a_apply[9])], sem="sD2")
                c_d2b[0] = dma("sync", D2[64:128, :], disha_d, sem="sD2")
                emit_wdma(0)

        c_hs = dma("sync", hsel[:], hsel_d, sem="sCst")

        for ec in range(ECH):
            mk_weff(ec)
            if ec + 4 < ECH:
                mk_wT(ec + 4)

        emit_wdma(1)

        def c_group(mms, evac_q, evac_fn, extra_waits=()):
            nonlocal gi
            par = gi % 4
            nmm = len(mms)
            pw = list(extra_waits)
            if evac_psC[par] is not None:
                pw.append(evac_psC[par])
            pval = None
            for i, mk in enumerate(mms):
                pval = S.emit("tensor", (lambda g, mk_=mk, p=par, first=(i == 0), last=(i == nmm - 1):
                                         mk_(g, psC[p], first, last)),
                              waits=(pw if i == 0 else []),
                              inc=("sPE" if i == nmm - 1 else None))
            ev = S.emit(evac_q, lambda g, p=par: evac_fn(g, psC[p]),
                        waits=[("sPE", pval)], inc=("sACT" if evac_q == "scalar" else "sDVE"))
            evac_psC[par] = (("sACT" if evac_q == "scalar" else "sDVE"), ev)
            gi += 1
            return pval

        for pi, (widx0, bias, scale) in enumerate([(0, bqcs, s125), (2, bkcs, None)]):
            dst = QT if pi == 0 else KT
            for half in range(2):
                widx = widx0 + half
                wpar = widx % 2
                last_p = None
                for tc4 in range(4):
                    for ocl in range(4):
                        oc = half * 4 + ocl
                        mms = []
                        for ec in range(ECH):
                            mms.append(lambda g, ps, first, last, w=wpar, ec_=ec, o_=ocl, t_=tc4:
                                       g.matmul(ps, wsb[w][:, ec_, o_ * NP:(o_ + 1) * NP],
                                                xnT[:, ec_, t_ * 512:(t_ + 1) * 512],
                                                start=first, stop=last))
                        ew = [("sACT", v_xnT[4 * tc4 + 3]), (f"sW{widx % 2}", c_w[widx]),
                              ("sDVE", max(v_bqcs, v_bkcs))]
                        if pi == 1 and half == 1:
                            ew.append(("sDVE", v_weff[ECH - 1]))

                        def ev(g, ps, d=dst, o_=oc, t_=tc4, b=bias, sc=scale):
                            s1 = sc[:, 0:1] if sc is not None else 1.0
                            return g.tensor_scalar(d[:, o_, t_ * 512:(t_ + 1) * 512], ps,
                                                   s1, b[:, o_:o_ + 1], ALU.mult, ALU.add)
                        last_p = c_group(mms, "vector", ev, ew)
                p_wdone[widx] = last_p
                if widx + 2 < 6:
                    emit_wdma(widx + 2)
        v_vevac = [None] * NTT
        for half in range(2):
            widx = 4 + half
            wpar = widx % 2
            last_p = None
            for tt in range(NTT):
                mms = []
                for ec in range(ECH):
                    mms.append(lambda g, ps, first, last, w=wpar, ec_=ec, t_=tt:
                               g.matmul(ps, xnT[:, ec_, t_ * NP:(t_ + 1) * NP],
                                        wsb[w][:, ec_, :], start=first, stop=last))
                mms.append(lambda g, ps, first, last, h=half:
                           g.matmul(ps, ones128[0:1, :], bv_sb[0:1, h * 512:(h + 1) * 512],
                                    start=False, stop=last))
                ew = [("sACT", v_xnT[tt]), (f"sW{widx % 2}", c_w[widx]), ("sCst", c_cst),
                      ("sDVE", v_vones)]

                def ev(g, ps, t_=tt, h=half):
                    pin = ps.rearrange("p (a b) -> p a b", b=64)
                    po = bass.AP(Vp.ap().tensor, Vp.ap().offset + t_ * 1040 + h * 520,
                                 [list(Vp.ap().ap[0]), [65, 8], [1, 64]])
                    return g.tensor_copy(po, pin)
                last_p = c_group(mms, "vector", ev, ew)
                v_vevac[tt] = evac_psC[(gi - 1) % 2][1]
            p_wdone[widx] = last_p
        g_done = {}
        a_gdone = None
        p_gmm = None
        for tt in range(NTT):
            for half in range(2):
                mms = []
                for ec in range(ECH):
                    mms.append(lambda g, ps, first, last, ec_=ec, t_=tt, h=half:
                               g.matmul(ps, xnT[:, ec_, t_ * NP:(t_ + 1) * NP],
                                        Weff[:, ec_, h * 512:(h + 1) * 512],
                                        start=first, stop=last))
                ew = [("sACT", v_xnT[tt]), ("sDVE", v_weff[ECH - 1])]

                def ev(g, ps, t_=tt, h=half):
                    return g.activation(g_sb[:, t_, h * 512:(h + 1) * 512], ps, AF.Silu)
                p_gmm = c_group(mms, "scalar", ev, ew)
                a_gdone = evac_psC[(gi - 1) % 4][1]
                g_done[(tt, half)] = ("sACT", a_gdone)

        c_wo = dma("sync", woT[:], woT_d.rearrange("(ec p) o -> p ec o", p=NP),
                   waits=[("sPE", p_gmm)], sem="sWO")

        dbg_gate = None
        if debug:
            for nm, sb in [("dbg_xnT", xnT), ("dbg_QT", QT), ("dbg_KT", KT),
                           ("dbg_V", Vp)]:
                dbg_gate = dma("sync", dbg[nm], sb[:],
                               waits=[("sACT", a_gdone), ("sDVE", v_vevac[NTT - 1]),
                                      ("sPE", p_gmm)], sem="sDBG")

        pt_last_pv = [None] * 7
        psp_last_exp = [None, None]
        last_evac = [("sACT", v_xnT[NTT - 1])]
        last_evac_gp = [None]
        sh_last = [None]
        tc_norm = [None] * NTC
        norm_meta = [None] * NTC
        n_left = [NPAIR] * NTC
        lnq = []
        gather_gate = [None, None]
        bc_gate = [None, None]
        gp = 0
        u = 0
        pvq = []

        psF = [Q2[:, i, :] for i in range(2)]
        fill_evac = [None, None]
        qt_free = [None] * NTC
        xe_done = [None] * NTC
        xe_ap = {}
        last_out = [None, None]
        FS = dict(fi=0, es=0, credit=0.0, evac=None, out=None, cur=None, cur_kind=None)
        CHUNK = 440.0
        CCAP = 2600.0

        fillq = []
        for tcq in range(NTC):
            for fp in range(NPAIR):
                fillq.append(dict(kind="N", tc=tcq, p=fp, cost=440.0,
                                  ready=(lambda t=tcq: lambda: norm_meta[t] is not None)()))
            for ft in range(4 * tcq, 4 * tcq + 4):
                for fh in range(2):
                    fillq.append(dict(kind="E", tt=ft, h=fh, cost=1925.0,
                                      ready=(lambda t=tcq: lambda: tc_norm[t] is not None)()))

        def mk_n_tile(ent):
            pp, tcq = ent["p"], ent["tc"]
            rv, shv, b_ = norm_meta[tcq]
            tb_ = tcq * 512
            slot = FS["fi"] % 2
            FS["fi"] += 1
            pw = [("sDVE", rv), ("sCst", c_hs)]
            if fill_evac[slot] is not None:
                pw.append(fill_evac[slot])

            def em(sl=slot, pp_=pp, b2=b_, pw_=pw, tcq_=tcq, shv_=shv, t2=tb_):
                pval = S.emit("tensor", lambda g: g.matmul(
                    psF[sl], hsel[0:16, pp_ * NP:(pp_ + 1) * NP],
                    L16[0:16, b2, :],
                    start=True, stop=True), waits=pw_, inc="sPE")
                n_left[tcq_] -= 1
                if pp_ == NPAIR - 1:
                    bc_gate[b2] = ("sPE", pval)

                def evac():
                    mv_ = S.emit("vector", lambda g: g.tensor_tensor(
                        AttnOT[:, pp_, t2:t2 + 512], AttnOT[:, pp_, t2:t2 + 512],
                        psF[sl], ALU.mult), waits=[("sPE", pval), ("sSH0", shv_)],
                        inc="sDVE")
                    fill_evac[sl] = ("sDVE", mv_)
                    if pp_ == NPAIR - 1:
                        tc_norm[tcq_] = (mv_, shv_)
                    return None
                FS["evac"] = evac
            return [em]

        def mk_e_tile(ent):
            tt, oh = ent["tt"], ent["h"]
            tc_ = tt // 4
            if xe_done[tc_] is None:
                for t2 in range(4):
                    for h2 in range(2):
                        ap = QT[:, 2 * t2 + h2, tc_ * 512:(tc_ + 1) * 512]
                        xe_ap[(4 * tc_ + t2, h2)] = ap
                        dma("sync", ap, x_d[(4 * tc_ + t2) * NP:(4 * tc_ + t2 + 1) * NP,
                                            h2 * 512:(h2 + 1) * 512],
                            waits=[("sPE", qt_free[tc_])], sem=f"sX{tc_}")
                xe_done[tc_] = S.cnt[f"sX{tc_}"]
            slot = FS["fi"] % 2
            FS["fi"] += 1
            pw = [("sDVE", tc_norm[tc_][0]), ("sSH0", tc_norm[tc_][1]),
                  ("sWO", c_wo), ("sCst", c_cst)]
            if fill_evac[slot] is not None:
                pw.append(fill_evac[slot])
            ems = []
            for pc in range(ECH + 1):
                def em(pc_=pc, t_=tt, o=oh, sl=slot, tcq=tc_, first=(pc == 0),
                       last=(pc == ECH), pw_=pw):
                    if not last:
                        S.emit("tensor", lambda g: g.matmul(
                            psF[sl], AttnOT[:, pc_, t_ * NP:(t_ + 1) * NP],
                            woT[:, pc_, o * 512:(o + 1) * 512],
                            start=first, stop=False), waits=(pw_ if first else []))
                        return
                    pval = S.emit("tensor", lambda g: g.matmul(
                        psF[sl], ones128[0:1, :], bo_sb[0:1, o * 512:(o + 1) * 512],
                        start=False, stop=True), inc="sPE")

                    def evac():
                        es = FS["es"] % 2
                        FS["es"] += 1
                        wv_ = [("sPE", pval), g_done[(t_, o)]]
                        if last_out[es] is not None:
                            wv_.append((f"sO{es}", last_out[es]))
                        vtm = S.emit("vector", lambda g: g.tensor_tensor(
                            esb[es][:], psF[sl], g_sb[:, t_, o * 512:(o + 1) * 512],
                            ALU.mult), waits=wv_, inc="sDVE")
                        vadd = S.emit("vector", lambda g: g.tensor_tensor(
                            esb[es][:], esb[es][:], xe_ap[(t_, o)], ALU.add),
                            waits=[(f"sX{tcq}", xe_done[tcq])], inc="sDVE")
                        fill_evac[sl] = ("sDVE", vtm)

                        def out():
                            last_out[es] = S.emit("gpsimd", lambda g: g.dma_start(
                                out=out_d[t_ * NP:(t_ + 1) * NP, o * 512:(o + 1) * 512],
                                in_=esb[es][:]), waits=[("sDVE", vadd)], inc=f"sO{es}",
                                amt=16)
                        return out
                    FS["evac"] = evac
                ems.append(em)
            return ems

        def flush_pend():
            if FS["out"] is not None:
                FS["out"]()
                FS["out"] = None
            if FS["evac"] is not None:
                FS["out"] = FS["evac"]()
                FS["evac"] = None

        def pop_chunk():
            if FS["cur"] is None:
                if not fillq:
                    return False
                if not fillq[0]["ready"]():
                    flush_pend()
                    if not fillq[0]["ready"]():
                        return False
                ent = fillq.pop(0)
                flush_pend()
                FS["cur"] = (mk_n_tile if ent["kind"] == "N" else mk_e_tile)(ent)
                FS["cur_kind"] = ent["kind"]
            for _ in range(2):
                if FS["cur"]:
                    FS["cur"].pop(0)()
            if not FS["cur"]:
                FS["cur"] = None
                FS["cur_kind"] = None
            return True

        def psO(lo, hi, h, o0, o1):
            return PS6[lo:hi, 4 + h, o0:o1]

        def d_epilogue(p, tb, p_pv_val):
            nonlocal gp
            tcq = tb // 512
            wr = [("sPE", p_pv_val),
                  ("sSH0", sh_last[0]),
                  ("sLA", S.cnt.get("sLA")), ("sLB", S.cnt.get("sLB"))]
            evA = S.emit("vector", lambda g: g.tensor_copy(
                OddA[0:65, :], PS6[0:65, 4, :]), waits=wr, inc="sDVE")
            evB = S.emit("vector", lambda g: g.tensor_copy(
                OddB[0][0:65, :], PS6[0:65, 5, :]), inc="sDVE")
            last_evac[0] = evB
            wsh = [("sDVE", evB)]
            if dbg_gate is not None:
                wsh.append(("sDBG", dbg_gate))
            S.emit("sync", lambda g, p_=p, t=tb: g.dma_start(
                out=AttnOT[0:64, p_, t:t + 512], in_=OddA[0:64, :]),
                waits=wsh, inc="sSH0", amt=16)
            sh_last[0] = S.emit("sync", lambda g, p_=p, t=tb: g.dma_start(
                out=AttnOT[64:128, p_, t:t + 512], in_=OddB[0][0:64, :]),
                inc="sSH0", amt=16)
            b = (gp // 8) % 2
            semx = "sLA" if b == 0 else "sLB"
            for h in range(2):
                srcb = OddA if h == 0 else OddB[0]
                src = srcb[64:65, :]
                dstp = (2 * (gp % 8) + h)
                dst = bass.AP(L16.ap().tensor, dstp * 2048 + b * 512, [[2048, 1], [1, 512]])
                wg = [("sDVE", evB)]
                if gather_gate[b] is not None:
                    wg.append(gather_gate[b])
                if bc_gate[b] is not None:
                    wg.append(bc_gate[b])
                cg = S.emit("sync", lambda g, d=dst, sp=src: g.dma_start(out=d, in_=sp),
                            waits=wg, inc=semx, amt=16)
            if gp % 8 == 7:
                def mk_aux(b_=b, semx_=semx, tc_=gp // 8, gval=S.cnt[semx]):
                    def go():
                        wt = [(semx_, gval)]
                        if bc_gate[b_] is not None:
                            wt.append(bc_gate[b_])
                        def _recip(g):
                            with nc.allow_low_precision(reason="1/l in f16; error budget analysed"):
                                return g.reciprocal(L16[0:16, b_, :], L16[0:16, b_, :])
                        rv = S.emit("vector", _recip, waits=wt, inc="sDVE")
                        gather_gate[b_] = ("sDVE", rv)
                        norm_meta[tc_] = (rv, sh_last[0], b_)
                    return go
                lnq.append([2, mk_aux()])
            gp += 1

        for tc in range(NTC):
            tb = tc * 512
            nsc = 4 * tc + 4
            if tc >= 2:
                while n_left[tc - 2] > 0:
                    assert pop_chunk()
            for p in range(NPAIR):
                for sc in range(nsc):
                    for ent in lnq:
                        ent[0] -= 1
                    if lnq and lnq[0][0] <= 0:
                        lnq.pop(0)[1]()
                    par = u % 2
                    slot = u % 7
                    off = 0 if sc < 4 * tc else (sc - 4 * tc) * NP
                    N = 512 - off
                    diag = sc >= 4 * tc
                    sb_ = sc * NP
                    w0 = [("sACT", a_gdone)]
                    if u < 4:
                        w0.append(("sACT", v_xnT[NTT - 1]))
                    if psp_last_exp[par] is not None:
                        w0.append(("sACT", psp_last_exp[par]))
                    S.emit("tensor", lambda g, p_=p, par_=par, sb2=sb_, o=off, n=N, t=tb:
                           g.matmul(PS6[:, 2 * par_ + 0, o:o + n], KT[0:64, p_, sb2:sb2 + NP],
                                    QT[0:64, p_, t + o:t + 512], start=True, stop=True),
                           waits=w0)
                    p_sc = S.emit("tensor", lambda g, p_=p, par_=par, sb2=sb_, o=off, n=N, t=tb:
                                  g.matmul(PS6[:, 2 * par_ + 1, o:o + n], KT[64:128, p_, sb2:sb2 + NP],
                                           QT[64:128, p_, t + o:t + 512], start=True, stop=True),
                                  inc="sPE")
                    if p == NPAIR - 1 and sc == nsc - 1:
                        qt_free[tc] = p_sc
                    we = [("sPE", p_sc)]
                    if pt_last_pv[slot] is not None:
                        we.append(("sPE", pt_last_pv[slot]))
                    a_exp = S.emit("scalar", lambda g, par_=par, sl_=slot, o=off, n=N:
                                   g.activation(PT[sl_][:, :, o:o + n],
                                                PS6[:, 2 * par_:2 * par_ + 2, o:o + n],
                                                AF.Exp, bias=neg2[:]), waits=we, inc="sACT")
                    psp_last_exp[par] = a_exp
                    v_tri = None
                    if diag:
                        tb_ap = bass.AP(tri_sb.ap().tensor, tri_sb.ap().offset,
                                        [list(tri_sb.ap().ap[0]), [0, 2], [1, NP]])
                        v_tri = S.emit("gpsimd", lambda g, sl_=slot, o=off, t=tb_ap:
                                       g.tensor_tensor(PT[sl_][:, :, o:o + NP],
                                                       PT[sl_][:, :, o:o + NP],
                                                       t, ALU.mult),
                                       waits=[("sACT", a_exp), ("sCst", c_cst)], inc="sGP")

                    def mk_pv(p_=p, tb_=tb, sl_=slot, o=off, n=N, a_exp_=a_exp, v_tri_=v_tri,
                              vv_=v_vevac[sc], s_=sc,
                              first=(sc == 0), last=(sc == nsc - 1)):
                        def go():
                            wp = [("sACT", a_exp_), ("sDVE", vv_)]
                            if v_tri_ is not None:
                                wp.append(("sGP", v_tri_))
                            if first and last_evac[0] is not None:
                                wp.append(last_evac[0] if isinstance(last_evac[0], tuple)
                                          else ("sDVE", last_evac[0]))
                            if first and last_evac_gp[0] is not None:
                                wp.append(("sGP", last_evac_gp[0]))
                            S.emit("tensor", lambda g: g.matmul(
                                psO(0, 65, 0, o, o + n),
                                Vp[:, s_, (2 * p_) * 65:(2 * p_) * 65 + 65],
                                PT[sl_][:, 0, o:o + n], start=first, stop=last), waits=wp)
                            pv = S.emit("tensor", lambda g: g.matmul(
                                psO(0, 65, 1, o, o + n),
                                Vp[:, s_, (2 * p_ + 1) * 65:(2 * p_ + 1) * 65 + 65],
                                PT[sl_][:, 1, o:o + n], start=first, stop=last), inc="sPE")
                            pt_last_pv[sl_] = pv
                            if last:
                                d_epilogue(p_, tb_, pv)
                        return go
                    pv_e = mk_pv()
                    if len(pvq) >= 5:
                        pvq.pop(0)()
                    pvq.append(pv_e)
                    u += 1
                    FS["credit"] = min(CCAP, FS["credit"]
                                       + (172.0 + 2 * N) / 1.2 - 3.8 * N / 2.4)
                    while FS["credit"] >= CHUNK:
                        if not pop_chunk():
                            break
                        FS["credit"] -= CHUNK
        while pvq:
            pvq.pop(0)()
            while lnq:
                lnq.pop(0)[1]()

        while fillq or FS["cur"] is not None:
            assert pop_chunk(), "filler not ready at drain"
        flush_pend()
        flush_pend()
        for i2 in range(2):
            S.emit("gpsimd", lambda g, i_=i2: g.wait_ge(sems[f"sO{i_}"], last_out[i_]))

        if debug:
            dma("sync", dbg["dbg_AOT"], AttnOT[:],
                waits=[("sDVE", tc_norm[3][0]), ("sSH0", sh_last[0])], sem="sDBG")
            gw = [(s, max((v for s2, v in g_done.values() if s2 == s), default=None))
                  for s in ("sACT", "sDVE")]
            dma("sync", dbg["dbg_g"], g_sb[:], waits=gw, sem="sDBG")
            S.emit("sync", lambda g: g.wait_ge(sems["sDBG"], S.cnt["sDBG"]))

        globals()["_last_cnt"] = dict(S.cnt)
        def run(eng, q):
            lastw = {}
            for fn, waits, inc, amt, _val in S.ops[q]:
                for (s, v) in waits:
                    if lastw.get(s, -1) >= v:
                        continue
                    eng.wait_ge(sems[s], v)
                    lastw[s] = v
                inst = fn(eng)
                if inc is not None:
                    inst.then_inc(sems[inc], amt)

        @block.sync
        def _(eng):
            run(eng, "sync")

        @block.tensor
        def _(eng):
            run(eng, "tensor")

        @block.vector
        def _(eng):
            run(eng, "vector")

        @block.scalar
        def _(eng):
            run(eng, "scalar")

        @block.gpsimd
        def _(eng):
            run(eng, "gpsimd")

    return nc


def _mk_hsel():
    hs = np.zeros((16, E), dtype=np.float16)
    for p in range(NPAIR):
        hs[2 * p, p * NP:p * NP + 64] = 1
        hs[2 * p + 1, p * NP + 64:p * NP + NP] = 1
    return hs


def _pad16(b):
    out = np.zeros((16, NP), dtype=np.float16)
    out[0:ECH] = np.asarray(b).reshape(ECH, NP)
    return out


def make_inputs(inputs):
    f16 = np.float16
    x = np.asarray(inputs["L0_attn_hidden_states"])
    wq = np.asarray(inputs["L0_attn_wq"])
    wk = np.asarray(inputs["L0_attn_wk"])
    wv = np.asarray(inputs["L0_attn_wv"])
    wo = np.asarray(inputs["L0_attn_wo"])
    bq = np.asarray(inputs["L0_attn_bq"])
    bk = np.asarray(inputs["L0_attn_bk"])
    bv = np.asarray(inputs["L0_attn_bv"])
    bo = np.asarray(inputs["L0_attn_bo"])
    w = np.asarray(inputs["L0_ffn_weight"])
    disha = np.asarray(inputs["L0_ffn_disha"])
    common = dict(
        wqT=np.ascontiguousarray(wq.T), wkT=np.ascontiguousarray(wk.T),
        wvT=np.ascontiguousarray(wv.T), woT=np.ascontiguousarray(wo.T),
        wT=np.ascontiguousarray(w.T.astype(np.float32)),
        disha=np.ascontiguousarray(disha.astype(np.float32)),
        bq2=_pad16(bq), bk2=_pad16(bk),
        bv=np.ascontiguousarray(bv.reshape(1, E)),
        bo=np.ascontiguousarray(bo.reshape(1, E)),
        tri=np.triu(np.ones((NP, NP), dtype=f16)),
        eye=np.eye(NP, dtype=np.float32),
        hsel=_mk_hsel(),
    )
    return [dict(x=np.ascontiguousarray(x[b]), **common) for b in range(8)]


def kernel(**inputs):
    in_maps = make_inputs(inputs)
    nc = build_program()
    res = run_bass_kernel_spmd(nc, in_maps, core_ids=list(range(8)))
    return np.stack([res.results[b]["out"] for b in range(8)], axis=0)
